# version 04
# baseline (speedup 1.0000x reference)
import numpy as np
import ml_dtypes
from contextlib import ExitStack

import concourse.bass as bass
import concourse.mybir as mybir
from concourse.bass_utils import run_bass_kernel_spmd
from concourse.alu_op_type import AluOpType as Alu

F32 = mybir.dt.float32
BF16 = mybir.dt.bfloat16
AF = mybir.ActivationFunctionType
AX = mybir.AxisListType.X
BFNP = ml_dtypes.bfloat16

B, T, D = 4, 2048, 4096
CEXP = 32.0
SCALE = 1.0 / float(np.sqrt(128.0))
EPS = 1e-5
ROPE_BASE = 500000.0

JMIN = [g // 2 for g in range(16)]
CHUNKS = []
for g in range(16):
    off = 128 * JMIN[g]
    if off < 512:
        CHUNKS.append((g, off, 512 - off, True))
        CHUNKS.append((g, 512, 512, False))
    else:
        CHUNKS.append((g, off, 1024 - off, True))
NCH = len(CHUNKS)
CH_OF_G = {}
for k, (g, off, n, first) in enumerate(CHUNKS):
    CH_OF_G.setdefault(g, []).append(k)
LAST_K = {0: CH_OF_G[7][0], 1: NCH - 1}


def sview(tsr, off, tail, nparts=128):
    ap = tsr[:, :]
    return bass.AP(ap.tensor, ap.offset + off, [[ap.ap[0][0], nparts]] + tail)


def build_nc(upto="full"):
    nc = bass.Bass("TRN2", target_bir_lowering=False, detect_race_conditions=False)

    def di(name, shape, dt=F32):
        return nc.dram_tensor(name, shape, dt, kind="ExternalInput").ap()

    xo = di("xo", [1024, D])
    wq = di("wq", [32, 128, 32, 128])
    wk = di("wk", [8, 128, 32, 128])
    wv = di("wv", [8, 128, 32, 128])
    wo = di("wo", [32, 128, 32, 128])
    bwt = di("bwt", [32, 128, 32, 128])
    la = di("la", [128, 32, 128])
    lb = di("lb", [128, 32, 128])
    rwt = di("rwt", [128, 32, 8])
    crop = di("crop", [128, 1024])
    srop = di("srop", [128, 1024])
    m0 = di("m0", [128, 128], BF16)
    m1 = di("m1", [128, 128], BF16)
    identb = di("identb", [128, 128], BF16)
    identf = di("identf", [128, 128], F32)
    ones128 = di("ones128", [128, 128], BF16)
    onesrow = di("onesrow", [128, 128], F32)
    rep = di("rep", [8, 128], F32)
    cvec = di("cvec", [128, 4], F32)

    dbgA = (upto == "A")
    dbgB = (upto == "B")
    dbgC = (upto == "C")
    kc_d = nc.dram_tensor("kc_d", [8, 128, 1024], BF16, kind="Internal").ap()
    vc_d = nc.dram_tensor("vc_d", [8, 128, 1024], BF16, kind="Internal").ap()
    kg_d = nc.dram_tensor("kg_d", [2, 8, 128, 1024], BF16, kind="Internal").ap()
    vg_d = nc.dram_tensor("vg_d", [2, 8, 128, 1024], BF16, kind="Internal").ap()
    kvg_o = (nc.dram_tensor("kvg_o", [2, 16, 128, 1024], BF16, kind="ExternalOutput").ap()
             if dbgA else None)
    ctx_d = sg_d = out = wbig_d = mid_d = None
    if upto != "A":
        ctx_d = nc.dram_tensor("ctx_d", [32, 128, 1024], BF16,
                               kind="ExternalOutput" if dbgB else "Internal").ap()
    if upto in ("full", "C"):
        sg_d = nc.dram_tensor("sg_d", [32, 128, 1024], BF16,
                              kind="ExternalOutput" if dbgC else "Internal").ap()
        out = nc.dram_tensor("out", [1024, D], F32, kind="ExternalOutput").ap()
        if dbgC:
            wbig_d = nc.dram_tensor("wbig_d", [128, 1024], F32, kind="ExternalOutput").ap()
            mid_d = nc.dram_tensor("mid_d", [128, 1024], BF16, kind="ExternalOutput").ap()

    est = ExitStack()
    with est:
        sems = {}
        for name in ["scst", "sx0", "sx1", "sveps", "sstd", "sln", "stpg", "stpe",
                     "skvmm", "skev", "svev", "skd0", "skd1", "swkv0", "swkv1", "svtg",
                     "svte", "svd0", "svd1", "sbar", "scck", "sccv",
                     "swq0", "swq1", "sq", "sqr", "ssmm", "sexp", "smsk",
                     "sav", "scs", "srec", "sbc", "sctx", "sctd0", "sctd1", "sblk", "sblv", "sbld",
                     "sd0", "sd0dma", "swo0", "swo1", "sbw0", "sbw1", "swof", "ssC", "sfC",
                     "ssev", "ssil", "ssg", "ssgd0", "ssgd1", "sc0",
                     "ssgi0", "ssgi1", "st2", "soadd", "sout0", "sout1"]:
            sems[name] = est.enter_context(nc.semaphore(name))
        S = sems

        def gsb(name, shape, dt):
            return est.enter_context(nc.sbuf_tensor(name, shape, dt))

        crop_s = gsb("crop_s", [128, 1024], F32)
        srop_s = gsb("srop_s", [128, 1024], F32)
        m0_s = gsb("m0_s", [128, 128], BF16)
        m1_s = gsb("m1_s", [128, 128], BF16)
        identb_s = gsb("identb_s", [128, 128], BF16)
        identf_s = gsb("identf_s", [128, 128], F32)
        ones_s = gsb("ones_s", [128, 128], BF16)
        onesr_s = gsb("onesr_s", [128, 128], F32)
        rep_s = gsb("rep_s", [8, 128], F32)
        cvec_s = gsb("cvec_s", [128, 4], F32)
        CB_EXP = cvec_s[:, 0:1]
        CB_ZERO = cvec_s[:, 1:2]
        CB_EPS = cvec_s[:, 3:4]
        nc.const_aps.register(F32, 0.0, cvec_s[:, 1:2])
        nc.const_aps.register(F32, 1.0, cvec_s[:, 2:3])
        xnfm = gsb("xnfm", [128, 32 * 1024], BF16)
        lb_s = gsb("lb_s", [128, 4096], BF16)
        mid_s = gsb("mid_s", [128, 1024], BF16)
        wbig_s = gsb("wbig_s", [128, 1024], F32)

        pb = [est.enter_context(nc.psum_tensor("pb%d" % i, [128, 512], F32))
              for i in range(8)]

        CONST_DMAS = 10

        stD = ExitStack()
        la_s = stD.enter_context(nc.sbuf_tensor("la_s", [128, 4096], BF16))
        rwt_s = stD.enter_context(nc.sbuf_tensor("rwt_s", [128, 256], BF16))
        lg_sb = stD.enter_context(nc.sbuf_tensor("lg_sb", [8, 1024], F32))
        lgt_sb = stD.enter_context(nc.sbuf_tensor("lgt_sb", [128, 64], F32))
        wdt_sb = stD.enter_context(nc.sbuf_tensor("wdt_sb", [128, 64], F32))
        wdf_sb = stD.enter_context(nc.sbuf_tensor("wdf_sb", [8, 1024], F32))
        srt8 = stD.enter_context(nc.sbuf_tensor("srt8", [128, 8], F32))
        m1bb = stD.enter_context(nc.sbuf_tensor("m1bb", [128, 8], F32))
        m2bb = stD.enter_context(nc.sbuf_tensor("m2bb", [128, 8], F32))
        d1b = stD.enter_context(nc.sbuf_tensor("d1b", [128, 8], F32))
        d2b = stD.enter_context(nc.sbuf_tensor("d2b", [128, 8], F32))
        m2nn = stD.enter_context(nc.sbuf_tensor("m2nn", [128, 8], F32))
        recs = stD.enter_context(nc.sbuf_tensor("recs", [128, 1], F32))
        m1s = stD.enter_context(nc.sbuf_tensor("m1s", [128, 1], F32))
        m2s = stD.enter_context(nc.sbuf_tensor("m2s", [128, 1], F32))
        dts = stD.enter_context(nc.sbuf_tensor("dts", [128, 1], F32))
        w1s = stD.enter_context(nc.sbuf_tensor("w1s", [128, 1], F32))
        w2s = stD.enter_context(nc.sbuf_tensor("w2s", [128, 1], F32))
        e1s = stD.enter_context(nc.sbuf_tensor("e1s", [128, 8], F32))
        e2s = stD.enter_context(nc.sbuf_tensor("e2s", [128, 8], F32))
        tts = stD.enter_context(nc.sbuf_tensor("tts", [128, 8], F32))

        stA = ExitStack()
        with stA:
            def asb(name, shape, dt):
                return stA.enter_context(nc.sbuf_tensor(name, shape, dt))
            x_in = [asb("x_in%d" % i, [128, D], F32) for i in range(2)]
            xn_tm = [asb("xn_tm%d" % i, [128, D], BF16) for i in range(2)]
            bno = asb("bno", [128, 48], F32)
            mvv = asb("mvv", [128, 2], F32)
            stdv = asb("stdv", [128, 1], F32)
            rstd = asb("rstd", [128, 1], F32)
            wkv_s = [asb("wkv_s%d" % i, [128, 4096], BF16) for i in range(2)]
            kst = [asb("kst%d" % i, [128, 1024], BF16) for i in range(2)]
            vfm = asb("vfm", [128, 8 * 1024], BF16)
            vst = [asb("vst%d" % i, [128, 1024], BF16) for i in range(2)]
            ra = asb("ra", [128, 512], F32)
            rb = asb("rb", [128, 512], F32)
            rs = asb("rs", [128, 512], F32)

            with nc.Block() as blk:
                @blk.sync
                def _(s):
                    for (dst, src) in [(crop_s, crop), (srop_s, srop),
                                       (m0_s, m0), (m1_s, m1),
                                       (identb_s, identb), (identf_s, identf),
                                       (ones_s, ones128), (onesr_s, onesrow),
                                       (rep_s, rep), (cvec_s, cvec)]:
                        s.dma_start(dst[:, :], src).then_inc(S["scst"], 16)
                    for j in range(8):
                        if j >= 2:
                            s.wait_ge(S["sln"], j - 1)
                        s.dma_start(x_in[j % 2][:, :], xo[128 * j:128 * j + 128, :]
                                    ).then_inc(S["sx%d" % (j % 2)], 16)
                    for g in range(8):
                        s.wait_ge(S["skev"], g + 1)
                        s.dma_start(kc_d[g], kst[g % 2][:, :]
                                    ).then_inc(S["skd%d" % (g % 2)], 16)
                    for j in range(8):
                        s.wait_ge(S["svte"], j + 1)
                        s.dma_start(vc_d[j], vst[j % 2][:, :]
                                    ).then_inc(S["svd%d" % (j % 2)], 16)

                @blk.gpsimd
                def _(g_):
                    for goc in range(16):
                        if goc >= 2:
                            g_.wait_ge(S["skvmm"], 2 * (goc - 1))
                        src4 = wk[goc] if goc < 8 else wv[goc - 8]
                        src = bass.AP(src4.tensor, src4.offset,
                                      [[src4.ap[0][0], 128], [1, 4096]])
                        g_.dma_start(wkv_s[goc % 2][:, :], src,
                                     max_dma_last_dim=8192).then_inc(S["swkv%d" % (goc % 2)], 16)
                    g_.dma_start(rwt_s[:, :],
                                 bass.AP(rwt.tensor, rwt.offset, [[rwt.ap[0][0], 128], [1, 256]]),
                                 max_dma_last_dim=8192).then_inc(S["sd0dma"], 16)
                    g_.dma_start(la_s[:, :],
                                 bass.AP(la.tensor, la.offset, [[la.ap[0][0], 128], [1, 4096]]),
                                 max_dma_last_dim=8192).then_inc(S["sd0dma"], 16)
                    g_.dma_start(lb_s[:, :],
                                 bass.AP(lb.tensor, lb.offset, [[lb.ap[0][0], 128], [1, 4096]]),
                                 max_dma_last_dim=8192).then_inc(S["sd0dma"], 16)
                    g_.wait_ge(S["skd0"], 64)
                    g_.wait_ge(S["skd1"], 64)
                    g_.collective_compute(
                        "AllGather", mybir.AluOpType.bypass,
                        replica_groups=[[0, 1], [2, 3], [4, 5], [6, 7]],
                        ins=[kc_d.opt()], outs=[kg_d.opt()],
                    ).then_inc(S["scck"], 1)
                    g_.wait_ge(S["svd0"], 64)
                    g_.wait_ge(S["svd1"], 64)
                    g_.collective_compute(
                        "AllGather", mybir.AluOpType.bypass,
                        replica_groups=[[0, 1], [2, 3], [4, 5], [6, 7]],
                        ins=[vc_d.opt()], outs=[vg_d.opt()],
                    ).then_inc(S["sccv"], 1)

                @blk.vector
                def _(v):
                    v.wait_ge(S["scst"], 16 * CONST_DMAS)

                    def ln(j):
                        v.wait_ge(S["sx%d" % (j % 2)], 16 * (j // 2 + 1))
                        xs = x_in[j % 2]
                        for u in range(8):
                            v.bn_stats(bno[:, 6 * u:6 * u + 6], xs[:, 512 * u:512 * u + 512])
                        v.drain()
                        v.bn_aggr(mvv[:, :], bno[:, :])
                        v.drain().then_inc(S["sveps"], 1)
                        v.wait_ge(S["sstd"], j + 1)
                        v.reciprocal(rstd[:, :], stdv[:, :])
                        v.drain()
                        if j >= 2:
                            v.wait_ge(S["stpg"], 8 * (j - 1))
                        v.tensor_scalar(xn_tm[j % 2][:, :], xs[:, :], mvv[:, 0:1],
                                        rstd[:, 0:1], Alu.subtract, Alu.mult
                                        ).then_inc(S["sln"], 1)

                    def ropeK(g):
                        pair = (pb[2], pb[3]) if g % 2 == 0 else (pb[4], pb[5])
                        if g >= 2:
                            v.wait_ge(S["skd%d" % (g % 2)], 16 * (g // 2))
                        for th in range(2):
                            v.wait_ge(S["skvmm"], 2 * g + th + 1)
                            ps = pair[th]
                            v.tensor_tensor(ra[:, :], ps[:, :],
                                            crop_s[:, 512 * th:512 * th + 512], Alu.mult)
                            v.tensor_copy(rs[0:64, :], ps[64:128, :])
                            v.tensor_copy(rs[64:128, :], ps[0:64, :])
                            v.tensor_tensor(rb[:, :], rs[:, :],
                                            srop_s[:, 512 * th:512 * th + 512], Alu.mult)
                            v.tensor_tensor(kst[g % 2][:, 512 * th:512 * th + 512],
                                            ra[:, :], rb[:, :], Alu.add)
                        v.drain().then_inc(S["skev"], 1)

                    def vstev(j):
                        v.wait_ge(S["svtg"], j + 1)
                        if j >= 2:
                            v.wait_ge(S["svd%d" % (j % 2)], 16 * (j // 2))
                        v.tensor_copy(vst[j % 2][:, 0:512], pb[6][:, :])
                        v.tensor_copy(vst[j % 2][:, 512:1024], pb[7][:, :])
                        v.drain().then_inc(S["svte"], 1)

                    for j in range(8):
                        ln(j)
                    for g in range(8):
                        ropeK(g)
                    for j in range(8):
                        vstev(j)

                @blk.scalar
                def _(a):
                    a.wait_ge(S["scst"], 16 * CONST_DMAS)

                    def sqrt_(j):
                        a.wait_ge(S["sveps"], j + 1)
                        a.activation(stdv[:, :], mvv[:, 1:2], AF.Sqrt,
                                     bias=CB_EPS, scale=1.0)
                        a.drain().then_inc(S["sstd"], 1)

                    def tevac_sub(j):
                        for grp in range(8):
                            gc = 8 * j + grp
                            a.wait_ge(S["stpg"], gc + 1)
                            dst = sview(xnfm, (4 * grp) * 1024 + 128 * j,
                                        [[1024, 4], [1, 128]])
                            a.copy(dst, pb[gc % 2][:, :]).then_inc(S["stpe"], 1)

                    sqrt_(0)
                    sqrt_(1)
                    for j in range(8):
                        tevac_sub(j)
                        if j + 2 <= 7:
                            sqrt_(j + 2)
                    for g in range(8):
                        pair = (pb[2], pb[3]) if g % 2 == 0 else (pb[4], pb[5])
                        for th in range(2):
                            a.wait_ge(S["skvmm"], 2 * (8 + g) + th + 1)
                            inst = a.copy(vfm[:, g * 1024 + 512 * th:g * 1024 + 512 * th + 512],
                                          pair[th][:, :])
                            if th == 1:
                                inst.then_inc(S["svev"], 1)

                @blk.tensor
                def _(t):
                    t.wait_ge(S["scst"], 16 * CONST_DMAS)
                    for j in range(8):
                        t.wait_ge(S["sln"], j + 1)
                        for grp in range(8):
                            gc = 8 * j + grp
                            if gc >= 2:
                                t.wait_ge(S["stpe"], gc - 1)
                            unit = pb[gc % 2]
                            inst = None
                            for u in range(4):
                                kf = 4 * grp + u
                                inst = t.matmul(unit[:, 128 * u:128 * u + 128],
                                                xn_tm[j % 2][:, 128 * kf:128 * kf + 128],
                                                identb_s[:, :], start=True, stop=True)
                            inst.then_inc(S["stpg"], 1)
                    t.wait_ge(S["stpe"], 64)
                    for goc in range(16):
                        t.wait_ge(S["swkv%d" % (goc % 2)], 16 * (goc // 2 + 1))
                        pgoc = goc - 2
                        if pgoc >= 0:
                            if pgoc < 8:
                                t.wait_ge(S["skev"], pgoc + 1)
                            else:
                                t.wait_ge(S["svev"], pgoc - 8 + 1)
                        pair = (pb[2], pb[3]) if goc % 2 == 0 else (pb[4], pb[5])
                        for kf in range(32):
                            for th in range(2):
                                inst = t.matmul(
                                    pair[th][:, :],
                                    wkv_s[goc % 2][:, 128 * kf:128 * kf + 128],
                                    xnfm[:, 1024 * kf + 512 * th:1024 * kf + 512 * th + 512],
                                    start=(kf == 0), stop=(kf == 31))
                                if kf == 31:
                                    inst.then_inc(S["skvmm"], 1)
                    t.wait_ge(S["svev"], 8)
                    for j in range(8):
                        if j >= 1:
                            t.wait_ge(S["svte"], j)
                        inst = None
                        for g in range(8):
                            unit = pb[6] if g < 4 else pb[7]
                            inst = t.matmul(unit[:, 128 * (g % 4):128 * (g % 4) + 128],
                                            vfm[:, g * 1024 + 128 * j:g * 1024 + 128 * j + 128],
                                            identb_s[:, :], start=True, stop=True)
                        inst.then_inc(S["svtg"], 1)

                @blk.sync
                def _(s):
                    s.nop().then_inc(S["sbar"], 1)

                @blk.gpsimd
                def _(g_):
                    g_.nop().then_inc(S["sbar"], 1)

                @blk.vector
                def _(v):
                    v.nop().then_inc(S["sbar"], 1)

                @blk.scalar
                def _(a):
                    a.nop().then_inc(S["sbar"], 1)

                @blk.tensor
                def _(t):
                    t.nop().then_inc(S["sbar"], 1)

        if upto == "A":
            with nc.Block() as blk:
                @blk.sync
                def _(s):
                    s.wait_ge(S["sbar"], 5)
                    s.wait_ge(S["scck"], 1)
                    s.wait_ge(S["sccv"], 1)
                    s.dma_start(kvg_o[:, 0:8], kg_d).then_inc(S["sbld"], 16)
                    s.dma_start(kvg_o[:, 8:16], vg_d).then_inc(S["sbld"], 16)
                    s.wait_ge(S["sbld"], 32)
            stD.close()
            return nc

        d0 = [0]

        def d0w(eng):
            eng.wait_ge(S["sd0"], d0[0])

        def d0i(eng):
            d0[0] += 1
            eng.drain().then_inc(S["sd0"], 1)

        with nc.Block() as blk:
            @blk.tensor
            def _(t):
                t.wait_ge(S["sbar"], 5)
                t.wait_ge(S["sd0dma"], 48)
                t.drain()
                for th in range(2):
                    inst = None
                    for kf in range(32):
                        inst = t.matmul(pb[th][0:8, :], rwt_s[:, 8 * kf:8 * kf + 8],
                                        xnfm[:, 1024 * kf + 512 * th:1024 * kf + 512 * th + 512],
                                        start=(kf == 0), stop=(kf == 31))
                    d0i(t)

            @blk.vector
            def _(v):
                v.wait_ge(S["sbar"], 5)
                d0w(v)
                v.tensor_copy(lg_sb[0:8, 0:512], pb[0][0:8, :])
                v.tensor_copy(lg_sb[0:8, 512:1024], pb[1][0:8, :])
                d0i(v)

            @blk.tensor
            def _(t):
                d0w(t)
                for i in range(8):
                    t.matmul(pb[2][:, 8 * i:8 * i + 8],
                             lg_sb[0:8, 128 * i:128 * i + 128],
                             identf_s[0:8, 0:8], start=True, stop=True)
                d0i(t)

            @blk.vector
            def _(v):
                d0w(v)
                v.tensor_copy(lgt_sb[:, :], pb[2][:, 0:64])
                d0i(v)

            for tb in range(8):
                @blk.vector
                def _(v, tb=tb):
                    d0w(v)
                    blkl = lgt_sb[:, 8 * tb:8 * tb + 8]
                    v.max(out=srt8[:, :], in_=blkl)
                    v.drain()
                    v.tensor_scalar(m1bb[:, :], blkl, 0.0, srt8[:, 0:1], Alu.mult, Alu.add)
                    v.tensor_scalar(m2bb[:, :], blkl, 0.0, srt8[:, 1:2], Alu.mult, Alu.add)
                    v.drain()
                    v.tensor_tensor(d1b[:, :], blkl, m1bb[:, :], Alu.subtract)
                    v.tensor_tensor(d2b[:, :], blkl, m2bb[:, :], Alu.subtract)
                    v.tensor_tensor(m2nn[:, :], m2bb[:, :], blkl, Alu.subtract)
                    v.drain()
                    v.tensor_scalar(d1b[:, :], d1b[:, :], 1e6, -100.0, Alu.mult, Alu.max)
                    v.tensor_tensor(d2b[:, :], d2b[:, :], m2nn[:, :], Alu.min)
                    v.drain()
                    v.tensor_scalar(d2b[:, :], d2b[:, :], 1e6, -100.0, Alu.mult, Alu.max)
                    v.tensor_tensor(dts[:, :], srt8[:, 1:2], srt8[:, 0:1], Alu.subtract)
                    d0i(v)

                @blk.scalar
                def _(a, tb=tb):
                    d0w(a)
                    a.activation(e1s[:, :], d1b[:, :], AF.Exp, bias=CB_ZERO, scale=1.0)
                    a.activation(e2s[:, :], d2b[:, :], AF.Exp, bias=CB_ZERO, scale=1.0)
                    a.activation(m1s[:, :], dts[:, :], AF.Exp, bias=CB_ZERO, scale=1.0)
                    d0i(a)

                @blk.vector
                def _(v, tb=tb):
                    d0w(v)
                    v.tensor_scalar(m2s[:, :], m1s[:, :], 1.0, None, Alu.add)
                    v.drain()
                    v.reciprocal(recs[:, :], m2s[:, :])
                    v.drain()
                    v.tensor_tensor(w2s[:, :], m1s[:, :], recs[:, :], Alu.mult)
                    v.drain()
                    v.tensor_scalar(w1s[:, :], w2s[:, :], -1.0, 1.0, Alu.mult, Alu.add)
                    v.drain()
                    v.tensor_scalar(tts[:, :], e1s[:, :], w1s[:, 0:1], None, Alu.mult)
                    v.drain()
                    v.scalar_tensor_tensor(wdt_sb[:, 8 * tb:8 * tb + 8], e2s[:, :],
                                           w2s[:, 0:1], tts[:, :], Alu.mult, Alu.add)
                    d0i(v)

            @blk.tensor
            def _(t):
                d0w(t)
                for i in range(8):
                    unit = pb[3] if i < 4 else pb[6]
                    t.matmul(unit[0:8, 128 * (i % 4):128 * (i % 4) + 128],
                             wdt_sb[:, 8 * i:8 * i + 8], identf_s[:, :],
                             start=True, stop=True)
                d0i(t)

            @blk.vector
            def _(v):
                d0w(v)
                for i in range(4):
                    v.tensor_copy(wdf_sb[0:8, 128 * i:128 * i + 128],
                                  pb[3][0:8, 128 * i:128 * i + 128])
                for i in range(4):
                    v.tensor_copy(wdf_sb[0:8, 512 + 128 * i:512 + 128 * i + 128],
                                  pb[6][0:8, 128 * i:128 * i + 128])
                d0i(v)

            @blk.tensor
            def _(t):
                d0w(t)
                for th in range(2):
                    t.matmul(pb[4 + th][:, :], rep_s[:, :],
                             wdf_sb[0:8, 512 * th:512 * th + 512],
                             start=True, stop=True)
                d0i(t)

            @blk.vector
            def _(v):
                d0w(v)
                v.tensor_copy(wbig_s[:, 0:512], pb[4][:, :])
                v.tensor_copy(wbig_s[:, 512:1024], pb[5][:, :])
                d0i(v)

            @blk.tensor
            def _(t):
                d0w(t)
                for th in range(2):
                    inst = None
                    for kf in range(32):
                        inst = t.matmul(pb[2 + th][:, :], la_s[:, 128 * kf:128 * kf + 128],
                                        xnfm[:, 1024 * kf + 512 * th:1024 * kf + 512 * th + 512],
                                        start=(kf == 0), stop=(kf == 31))
                    d0i(t)

            @blk.vector
            def _(v):
                d0w(v)
                v.tensor_tensor(mid_s[:, 0:512], pb[2][:, :], wbig_s[:, 0:512], Alu.mult)
                v.tensor_tensor(mid_s[:, 512:1024], pb[3][:, :],
                                wbig_s[:, 512:1024], Alu.mult)
                d0i(v)

        D0_FINAL = d0[0]
        stD.close()

        stB = ExitStack()
        with stB:
            def bsb(name, shape, dt):
                return stB.enter_context(nc.sbuf_tensor(name, shape, dt))
            k_sb = bsb("k_sb", [128, 8 * 2048], BF16)
            v_sb = bsb("v_sb", [128, 16 * 1024], BF16)
            wq_s = [bsb("wq_s%d" % i, [128, 4096], BF16) for i in range(2)]
            q_sb = bsb("q_sb", [128, 1024], BF16)
            p_sb = [bsb("p_sb%d" % i, [128, 1024], BF16) for i in range(2)]
            ra2 = bsb("ra2", [128, 512], F32)
            rb2 = bsb("rb2", [128, 512], F32)
            rs2 = bsb("rs2", [128, 512], F32)
            rz = bsb("rz", [128, 1024], F32)
            cte = bsb("cte", [128, 1024], F32)
            cxs = [bsb("cxs%d" % i, [128, 1024], BF16) for i in range(2)]

            with nc.Block() as blk:
                @blk.gpsimd
                def _(g_):
                    for h in range(32):
                        if h >= 2:
                            g_.wait_ge(S["sq"], 2 * (h - 1))
                        src = bass.AP(wq[h].tensor, wq[h].offset,
                                      [[wq[h].ap[0][0], 128], [1, 4096]])
                        g_.dma_start(wq_s[h % 2][:, :], src,
                                     max_dma_last_dim=8192).then_inc(S["swq%d" % (h % 2)], 16)

                @blk.sync
                def _(s):
                    s.wait_ge(S["sbar"], 5)
                    s.wait_ge(S["scck"], 1)
                    for r in range(2):
                        for g_ in range(8):
                            src = bass.AP(kg_d.tensor,
                                          kg_d.offset + (8 * r + g_) * (128 * 1024),
                                          [[1024, 128], [128, 8], [1, 128]])
                            dst = sview(k_sb, 2048 * g_ + 128 * r, [[256, 8], [1, 128]])
                            s.dma_start(dst, src).then_inc(S["sblk"], 16)
                    s.wait_ge(S["sccv"], 1)
                    for r in range(2):
                        src = bass.AP(vg_d.tensor,
                                      vg_d.offset + 8 * r * (128 * 1024),
                                      [[1024, 128], [131072, 8], [1, 1024]])
                        dst = sview(v_sb, 1024 * r, [[2048, 8], [1, 1024]])
                        s.dma_start(dst, src).then_inc(S["sblv"], 16)
                    for h in range(32):
                        s.wait_ge(S["sctx"], h + 1)
                        s.dma_start(ctx_d[h], cxs[h % 2][:, :]).then_inc(S["sctd%d" % (h % 2)], 16)

                SUNIT = [0, 1, 6]

                @blk.tensor
                def _(t):
                    t.wait_ge(S["sd0"], D0_FINAL)
                    for h in range(32):
                        gq = h // 4
                        if h == 0:
                            t.wait_ge(S["swq0"], 16)
                            t.wait_ge(S["sqr"], 0)
                        for kf in range(32):
                            for th in range(2):
                                inst = t.matmul(
                                    pb[6 + th][:, :],
                                    wq_s[h % 2][:, 128 * kf:128 * kf + 128],
                                    xnfm[:, 1024 * kf + 512 * th:1024 * kf + 512 * th + 512],
                                    start=(kf == 0), stop=(kf == 31))
                                if kf == 31:
                                    inst.then_inc(S["sq"], 1)
                        if h == 0:
                            t.wait_ge(S["sblk"], 16 * 16)
                        t.wait_ge(S["sqr"], h + 1)
                        started_r = set()
                        started_c = set()

                        def s_mm(k):
                            g_, off, n, first = CHUNKS[k]
                            ci = NCH * h + k
                            if ci >= 3:
                                t.wait_ge(S["sexp"], ci - 2)
                            t.matmul(pb[SUNIT[ci % 3]][:, 0:n],
                                     k_sb[:, 2048 * gq + 128 * g_:2048 * gq + 128 * g_ + 128],
                                     q_sb[:, off:off + n], start=True, stop=True
                                     ).then_inc(S["ssmm"], 1)

                        def cs_av(k):
                            g_, off, n, first = CHUNKS[k]
                            ci = NCH * h + k
                            if first:
                                t.wait_ge(S["smsk"], 16 * h + g_ + 1)
                            else:
                                t.wait_ge(S["sexp"], ci + 1)
                            u = off // 512
                            o2 = off % 512
                            inst = t.matmul(pb[4 + u][:, o2:o2 + n], ones_s[:, :],
                                            p_sb[g_ % 2][:, off:off + n],
                                            start=(u not in started_r),
                                            stop=(k == LAST_K[u]))
                            started_r.add(u)
                            if k == NCH - 1:
                                inst.then_inc(S["scs"], 1)
                            inst = t.matmul(
                                pb[2 + u][:, o2:o2 + n],
                                v_sb[:, 1024 * g_ + 128 * gq:1024 * g_ + 128 * gq + 128],
                                p_sb[g_ % 2][:, off:off + n],
                                start=(u not in started_c),
                                stop=(k == LAST_K[u]))
                            started_c.add(u)
                            inst.then_inc(S["sav"], 1)

                        s_mm(0)
                        s_mm(1)
                        if h == 0:
                            t.wait_ge(S["sblv"], 32)
                        for k in range(2, NCH):
                            s_mm(k)
                            cs_av(k - 2)
                        cs_av(NCH - 2)
                        cs_av(NCH - 1)
                        t.wait_ge(S["srec"], h + 1)
                        inst = None
                        for u in range(2):
                            inst = t.matmul(pb[6 + u][:, :], onesr_s[:, :],
                                            rz[:, 512 * u:512 * u + 512],
                                            start=True, stop=True)
                        inst.then_inc(S["sbc"], 1)
                        if h + 1 < 32:
                            t.wait_ge(S["swq%d" % ((h + 1) % 2)], 16 * ((h + 1) // 2 + 1))
                            t.wait_ge(S["sctx"], h + 1)
                            t.wait_ge(S["srec"], h + 1)

                @blk.scalar
                def _(a):
                    for h in range(32):
                        for k in range(NCH):
                            g_, off, n, first = CHUNKS[k]
                            ci = NCH * h + k
                            a.wait_ge(S["ssmm"], ci + 1)
                            if g_ >= 2:
                                a.wait_ge(S["sav"], NCH * h + CH_OF_G[g_ - 2][-1] + 1)
                            elif h >= 1:
                                a.wait_ge(S["sav"], NCH * (h - 1) + CH_OF_G[g_ + 14][-1] + 1)
                            a.activation(p_sb[g_ % 2][:, off:off + n],
                                         pb[[0, 1, 6][ci % 3]][:, 0:n],
                                         AF.Exp, bias=CB_EXP, scale=SCALE
                                         ).then_inc(S["sexp"], 1)

                @blk.vector
                def _(v):
                    v.memset(rz[:, :], 0.0)
                    for h in range(32):
                        for th in range(2):
                            v.wait_ge(S["sq"], 2 * h + th + 1)
                            ps = pb[6 + th]
                            v.tensor_tensor(ra2[:, :], ps[:, :],
                                            crop_s[:, 512 * th:512 * th + 512], Alu.mult)
                            v.tensor_copy(rs2[0:64, :], ps[64:128, :])
                            v.tensor_copy(rs2[64:128, :], ps[0:64, :])
                            v.tensor_tensor(rb2[:, :], rs2[:, :],
                                            srop_s[:, 512 * th:512 * th + 512], Alu.mult)
                            v.tensor_tensor(q_sb[:, 512 * th:512 * th + 512],
                                            ra2[:, :], rb2[:, :], Alu.add)
                        v.drain().then_inc(S["sqr"], 1)
                        for g_ in range(16):
                            kf_ = CH_OF_G[g_][0]
                            off = CHUNKS[kf_][1]
                            v.wait_ge(S["sexp"], NCH * h + kf_ + 1)
                            mt = m0_s if g_ % 2 == 0 else m1_s
                            v.tensor_tensor(p_sb[g_ % 2][:, off:off + 128],
                                            p_sb[g_ % 2][:, off:off + 128],
                                            mt[:, :], Alu.mult)
                            v.drain().then_inc(S["smsk"], 1)
                        v.wait_ge(S["scs"], h + 1)
                        if h >= 1:
                            v.wait_ge(S["sbc"], h)
                        v.reciprocal(rz[0:1, 0:512], pb[4][0:1, :])
                        v.reciprocal(rz[0:1, 512:1024], pb[5][0:1, :])
                        v.drain().then_inc(S["srec"], 1)
                        v.wait_ge(S["sav"], NCH * (h + 1))
                        v.wait_ge(S["sbc"], h + 1)
                        if h >= 2:
                            v.wait_ge(S["sctd%d" % (h % 2)], 16 * (h // 2))
                        v.tensor_copy(cte[:, 0:512], pb[2][:, :])
                        v.tensor_copy(cte[:, 512:1024], pb[3][:, :])
                        v.tensor_tensor(cxs[h % 2][:, 0:512], cte[:, 0:512],
                                        pb[6][:, :], Alu.mult)
                        v.tensor_tensor(cxs[h % 2][:, 512:1024], cte[:, 512:1024],
                                        pb[7][:, :], Alu.mult)
                        v.drain().then_inc(S["sctx"], 1)

                @blk.sync
                def _(s):
                    s.nop().then_inc(S["sbar"], 1)

                @blk.gpsimd
                def _(g_):
                    g_.nop().then_inc(S["sbar"], 1)

                @blk.vector
                def _(v):
                    v.nop().then_inc(S["sbar"], 1)

                @blk.scalar
                def _(a):
                    a.nop().then_inc(S["sbar"], 1)

                @blk.tensor
                def _(t):
                    t.nop().then_inc(S["sbar"], 1)

        if upto == "B":
            with nc.Block() as blk:
                @blk.sync
                def _(s):
                    s.wait_ge(S["sbar"], 10)
                    s.wait_ge(S["sctd0"], 16 * 16)
                    s.wait_ge(S["sctd1"], 16 * 16)
            return nc

        stC = ExitStack()
        with stC:
            def csb(name, shape, dt):
                return stC.enter_context(nc.sbuf_tensor(name, shape, dt))
            ctx_s = csb("ctx_s", [128, 32 * 1024], BF16)
            wo_s = [csb("wo_s%d" % i, [128, 4096], BF16) for i in range(2)]
            bwt_s = [csb("bwt_s%d" % i, [128, 4096], BF16) for i in range(2)]
            s_sb = csb("s_sb", [128, 1024], BF16)
            g_sb = [csb("g_sb%d" % i, [128, 1024], BF16) for i in range(2)]
            sg_sb = [csb("sg_sb%d" % i, [128, 1024], BF16) for i in range(2)]

            with nc.Block() as blk:
                @blk.gpsimd
                def _(g_):
                    for oc in range(32):
                        if oc >= 2:
                            g_.wait_ge(S["swof"], oc - 1)
                        g_.dma_start(wo_s[oc % 2][:, :],
                                     bass.AP(wo[oc].tensor, wo[oc].offset,
                                             [[wo[oc].ap[0][0], 128], [1, 4096]]),
                                     max_dma_last_dim=8192).then_inc(S["swo%d" % (oc % 2)], 16)
                        g_.dma_start(bwt_s[oc % 2][:, :],
                                     bass.AP(bwt[oc].tensor, bwt[oc].offset,
                                             [[bwt[oc].ap[0][0], 128], [1, 4096]]),
                                     max_dma_last_dim=8192).then_inc(S["sbw%d" % (oc % 2)], 16)

                @blk.sync
                def _(s):
                    s.wait_ge(S["sbar"], 10)
                    s.wait_ge(S["sctd0"], 16 * 16)
                    s.wait_ge(S["sctd1"], 16 * 16)
                    s.dma_start(ctx_s[:, :].rearrange("p (a c) -> p a c", a=32),
                                ctx_d.rearrange("a p c -> p a c")).then_inc(S["sc0"], 16)
                    if dbgC:
                        s.dma_start(wbig_d, wbig_s[:, :]).then_inc(S["sc0"], 16)
                        s.dma_start(mid_d, mid_s[:, :]).then_inc(S["sc0"], 16)
                    for oc in range(32):
                        s.wait_ge(S["ssg"], oc + 1)
                        s.dma_start(sg_d[oc], sg_sb[oc % 2][:, :]).then_inc(S["ssgd%d" % (oc % 2)], 16)

                @blk.tensor
                def _(t):
                    t.wait_ge(S["sbar"], 10)
                    t.wait_ge(S["sc0"], 16)
                    for oc in range(32):
                        t.wait_ge(S["swo%d" % (oc % 2)], 16 * (oc // 2 + 1))
                        t.wait_ge(S["sbw%d" % (oc % 2)], 16 * (oc // 2 + 1))
                        if oc >= 2:
                            t.wait_ge(S["ssev"], oc - 1)
                            t.wait_ge(S["ssil"], 2 * (oc - 1))
                        spair = (pb[0], pb[1]) if oc % 2 == 0 else (pb[4], pb[5])
                        fpair = (pb[2], pb[3]) if oc % 2 == 0 else (pb[6], pb[7])
                        for kf in range(32):
                            for th in range(2):
                                inst = t.matmul(
                                    spair[th][:, :], wo_s[oc % 2][:, 128 * kf:128 * kf + 128],
                                    ctx_s[:, 1024 * kf + 512 * th:1024 * kf + 512 * th + 512],
                                    start=(kf == 0), stop=(kf == 31))
                                if kf == 31:
                                    inst.then_inc(S["ssC"], 1)
                        for kf in range(32):
                            for th in range(2):
                                t.matmul(
                                    fpair[th][:, :], bwt_s[oc % 2][:, 128 * kf:128 * kf + 128],
                                    xnfm[:, 1024 * kf + 512 * th:1024 * kf + 512 * th + 512],
                                    start=(kf == 0), stop=False)
                        for th in range(2):
                            t.matmul(fpair[th][:, :], lb_s[:, 128 * oc:128 * oc + 128],
                                     mid_s[:, 512 * th:512 * th + 512],
                                     start=False, stop=True).then_inc(S["sfC"], 1)
                        t.nop().then_inc(S["swof"], 1)

                @blk.scalar
                def _(a):
                    for oc in range(32):
                        fpair = (pb[2], pb[3]) if oc % 2 == 0 else (pb[6], pb[7])
                        if oc >= 2:
                            a.wait_ge(S["ssg"], oc - 1)
                        for th in range(2):
                            a.wait_ge(S["sfC"], 2 * oc + th + 1)
                            a.activation(g_sb[oc % 2][:, 512 * th:512 * th + 512],
                                         fpair[th][:, :], AF.Silu, bias=CB_ZERO,
                                         scale=1.0).then_inc(S["ssil"], 1)

                @blk.vector
                def _(v):
                    for oc in range(32):
                        spair = (pb[0], pb[1]) if oc % 2 == 0 else (pb[4], pb[5])
                        v.wait_ge(S["ssC"], 2 * oc + 2)
                        v.tensor_copy(s_sb[:, 0:512], spair[0][:, :])
                        v.tensor_copy(s_sb[:, 512:1024], spair[1][:, :])
                        v.drain().then_inc(S["ssev"], 1)
                        v.wait_ge(S["ssil"], 2 * oc + 2)
                        if oc >= 2:
                            v.wait_ge(S["ssgd%d" % (oc % 2)], 16 * (oc // 2))
                        v.tensor_tensor(sg_sb[oc % 2][:, :], s_sb[:, :], g_sb[oc % 2][:, :],
                                        Alu.mult)
                        v.drain().then_inc(S["ssg"], 1)

                @blk.sync
                def _(s):
                    s.nop().then_inc(S["sbar"], 1)

                @blk.gpsimd
                def _(g_):
                    g_.nop().then_inc(S["sbar"], 1)

                @blk.vector
                def _(v):
                    v.nop().then_inc(S["sbar"], 1)

                @blk.scalar
                def _(a):
                    a.nop().then_inc(S["sbar"], 1)

                @blk.tensor
                def _(t):
                    t.nop().then_inc(S["sbar"], 1)

        stE = ExitStack()
        with stE:
            def esb(name, shape, dt):
                return stE.enter_context(nc.sbuf_tensor(name, shape, dt))
            sgi = [esb("sgi%d" % i, [128, 4096], BF16) for i in range(2)]
            xr = [esb("xr%d" % i, [128, 4096], F32) for i in range(2)]
            ob = [esb("ob%d" % i, [128, 4096], F32) for i in range(2)]

            with nc.Block() as blk:
                @blk.sync
                def _(s):
                    s.wait_ge(S["sbar"], 15)
                    s.wait_ge(S["ssgd0"], 16 * 16)
                    s.wait_ge(S["ssgd1"], 16 * 16)

                    def out_dma(tb):
                        s.wait_ge(S["soadd"], 8 * (tb + 1))
                        s.dma_start(out[128 * tb:128 * tb + 128, :], ob[tb % 2][:, :]
                                    ).then_inc(S["sout%d" % (tb % 2)], 16)

                    for tb in range(8):
                        if tb >= 2:
                            s.wait_ge(S["st2"], 8 * (tb - 1))
                        for half in range(2):
                            src = sg_d[16 * half:16 * half + 16, :, 128 * tb:128 * tb + 128]
                            s.dma_start(
                                sgi[tb % 2][:, 2048 * half:2048 * half + 2048].rearrange(
                                    "p (a c) -> p a c", a=16),
                                src.rearrange("a p c -> p a c")).then_inc(S["ssgi%d" % (tb % 2)], 16)
                        if tb >= 2:
                            s.wait_ge(S["soadd"], 8 * (tb - 1))
                        s.dma_start(xr[tb % 2][:, :], xo[128 * tb:128 * tb + 128, :]
                                    ).then_inc(S["ssgi%d" % (tb % 2)], 16)
                        if tb >= 2:
                            out_dma(tb - 2)
                    out_dma(6)
                    out_dma(7)
                    s.wait_ge(S["sout0"], 16 * 4)
                    s.wait_ge(S["sout1"], 16 * 4)

                @blk.tensor
                def _(t):
                    t.wait_ge(S["sbar"], 15)
                    for tb in range(8):
                        t.wait_ge(S["ssgi%d" % (tb % 2)], 48 * (tb // 2 + 1))
                        for grp in range(8):
                            if tb >= 1:
                                t.wait_ge(S["soadd"], 8 * (tb - 1) + grp + 1)
                            unit = pb[grp]
                            inst = None
                            for u in range(4):
                                oc = 4 * grp + u
                                inst = t.matmul(unit[:, 128 * u:128 * u + 128],
                                                sgi[tb % 2][:, 128 * oc:128 * oc + 128],
                                                identb_s[:, :], start=True, stop=True)
                            inst.then_inc(S["st2"], 1)

                @blk.vector
                def _(v):
                    for tb in range(8):
                        for grp in range(8):
                            cum = 8 * tb + grp
                            v.wait_ge(S["st2"], cum + 1)
                            if tb >= 2 and grp == 0:
                                v.wait_ge(S["sout%d" % (tb % 2)], 16 * (tb // 2))
                            v.tensor_tensor(ob[tb % 2][:, 512 * grp:512 * grp + 512],
                                            xr[tb % 2][:, 512 * grp:512 * grp + 512],
                                            pb[grp][:, :], Alu.add).then_inc(S["soadd"], 1)

    return nc


_NC_CACHE = {}


def _get_nc(upto="full"):
    if upto not in _NC_CACHE:
        _NC_CACHE[upto] = build_nc(upto)
    return _NC_CACHE[upto]


def make_inputs(L0_attn_x, L0_attn_Wq, L0_attn_Wk, L0_attn_Wv, L0_attn_Wo,
                L0_ffn_base_W, L0_ffn_router_W, L0_ffn_lora_A, L0_ffn_lora_B):
    f32 = np.float32
    wq_r = np.ascontiguousarray(np.asarray(L0_attn_Wq, f32).reshape(32, 128, 32, 128).transpose(2, 1, 0, 3))
    wk_r = np.ascontiguousarray(np.asarray(L0_attn_Wk, f32).reshape(32, 128, 8, 128).transpose(2, 1, 0, 3))
    wv_r = np.ascontiguousarray(np.asarray(L0_attn_Wv, f32).reshape(32, 128, 8, 128).transpose(2, 1, 0, 3))
    wo_r = np.ascontiguousarray(np.asarray(L0_attn_Wo, f32).reshape(32, 128, 32, 128).transpose(2, 1, 0, 3))
    bwt_r = np.ascontiguousarray(np.asarray(L0_ffn_base_W, f32).T.reshape(32, 128, 32, 128).transpose(2, 1, 0, 3))
    la_r = np.ascontiguousarray(np.asarray(L0_ffn_lora_A, f32).transpose(1, 0, 2).reshape(32, 128, 128).transpose(1, 0, 2))
    lb_r = np.ascontiguousarray(np.asarray(L0_ffn_lora_B, f32).reshape(128, 32, 128))
    rwt_r = np.ascontiguousarray(np.asarray(L0_ffn_router_W, f32).T.reshape(32, 128, 8).transpose(1, 0, 2))

    tri = np.tril(np.ones((128, 128), np.float32)).T
    identb_np = np.eye(128, dtype=BFNP)
    identf_np = np.eye(128, dtype=np.float32)
    ones128_np = np.ones((128, 128), BFNP)
    onesrow_np = np.ones((128, 128), np.float32)
    rep_np = np.zeros((8, 128), np.float32)
    for e in range(8):
        rep_np[e, 16 * e:16 * e + 16] = 2.0
    cvec_np = np.tile(np.array([[-CEXP, 0.0, 1.0, EPS]], np.float32), (128, 1))

    inv = (1.0 / ROPE_BASE ** (np.arange(0, 128, 2, dtype=np.float64) / 128.0))
    x = np.asarray(L0_attn_x, f32)

    in_maps = []
    for core in range(8):
        b, p = core // 2, core % 2
        ownrows = np.concatenate([np.arange(128 * (2 * j + p), 128 * (2 * j + p) + 128)
                                  for j in range(8)])
        xoc = np.ascontiguousarray(x[b][ownrows])
        pos = ownrows.astype(np.float64)
        ang = pos[None, :] * inv[:, None]
        cosv = np.cos(ang).astype(f32)
        sinv = np.sin(ang).astype(f32)
        crop_np = np.ascontiguousarray(np.concatenate([cosv, cosv], 0))
        srop_np = np.ascontiguousarray(np.concatenate([-sinv, sinv], 0))
        if p == 0:
            m0_np = tri.astype(BFNP)
            m1_np = np.zeros((128, 128), BFNP)
        else:
            m0_np = np.ones((128, 128), BFNP)
            m1_np = tri.astype(BFNP)
        in_maps.append({
            "xo": xoc, "wq": wq_r, "wk": wk_r, "wv": wv_r, "wo": wo_r,
            "bwt": bwt_r, "la": la_r, "lb": lb_r, "rwt": rwt_r,
            "crop": crop_np, "srop": srop_np, "m0": m0_np, "m1": m1_np,
            "identb": identb_np, "identf": identf_np, "ones128": ones128_np,
            "onesrow": onesrow_np, "rep": rep_np, "cvec": cvec_np,
        })
    return in_maps


def kernel(L0_attn_x, L0_attn_Wq, L0_attn_Wk, L0_attn_Wv, L0_attn_Wo,
           L0_ffn_base_W, L0_ffn_router_W, L0_ffn_lora_A, L0_ffn_lora_B):
    in_maps = make_inputs(L0_attn_x, L0_attn_Wq, L0_attn_Wk, L0_attn_Wv, L0_attn_Wo,
                          L0_ffn_base_W, L0_ffn_router_W, L0_ffn_lora_A, L0_ffn_lora_B)
    nc = _get_nc("full")
    res = run_bass_kernel_spmd(nc, in_maps, core_ids=list(range(8)))
    full = np.empty((B, T, D), np.float32)
    for core in range(8):
        b, p = core // 2, core % 2
        o = res.results[core]["out"]
        for j in range(8):
            gb = 2 * j + p
            full[b, 128 * gb:128 * gb + 128, :] = o[128 * j:128 * j + 128, :]
    return full


# version 05
# speedup vs baseline: 1.0585x; 1.0585x over previous
import numpy as np
import ml_dtypes
from contextlib import ExitStack

import concourse.bass as bass
import concourse.mybir as mybir
from concourse.bass_utils import run_bass_kernel_spmd
from concourse.alu_op_type import AluOpType as Alu

F32 = mybir.dt.float32
BF16 = mybir.dt.bfloat16
AF = mybir.ActivationFunctionType
AX = mybir.AxisListType.X
BFNP = ml_dtypes.bfloat16

B, T, D = 4, 2048, 4096
CEXP = 32.0
SCALE = 1.0 / float(np.sqrt(128.0))
EPS = 1e-5
ROPE_BASE = 500000.0

JMIN = [g // 2 for g in range(16)]
CHUNKS = []
for g in range(16):
    off = 128 * JMIN[g]
    if off < 512:
        CHUNKS.append((g, off, 512 - off, True))
        CHUNKS.append((g, 512, 512, False))
    else:
        CHUNKS.append((g, off, 1024 - off, True))
NCH = len(CHUNKS)
CH_OF_G = {}
for k, (g, off, n, first) in enumerate(CHUNKS):
    CH_OF_G.setdefault(g, []).append(k)
LAST_K = {0: CH_OF_G[7][0], 1: NCH - 1}


def sview(tsr, off, tail, nparts=128):
    ap = tsr[:, :]
    return bass.AP(ap.tensor, ap.offset + off, [[ap.ap[0][0], nparts]] + tail)


def build_nc(upto="full"):
    nc = bass.Bass("TRN2", target_bir_lowering=False, detect_race_conditions=False)

    def di(name, shape, dt=F32):
        return nc.dram_tensor(name, shape, dt, kind="ExternalInput").ap()

    xo = di("xo", [1024, D])
    wq = di("wq", [32, 128, 32, 128])
    wk = di("wk", [8, 128, 32, 128])
    wv = di("wv", [8, 128, 32, 128])
    wo = di("wo", [32, 128, 32, 128])
    bwt = di("bwt", [32, 128, 32, 128])
    la = di("la", [128, 32, 128])
    lb = di("lb", [128, 32, 128])
    rwt = di("rwt", [128, 32, 8])
    crop = di("crop", [128, 1024])
    srop = di("srop", [128, 1024])
    m0 = di("m0", [128, 128], BF16)
    m1 = di("m1", [128, 128], BF16)
    identb = di("identb", [128, 128], BF16)
    identf = di("identf", [128, 128], F32)
    ones128 = di("ones128", [128, 128], BF16)
    onesrow = di("onesrow", [128, 128], BF16)
    rep = di("rep", [8, 128], F32)
    cvec = di("cvec", [128, 4], F32)

    dbgA = (upto == "A")
    dbgB = (upto == "B")
    dbgC = (upto == "C")
    kc_d = nc.dram_tensor("kc_d", [8, 128, 1024], BF16, kind="Internal").ap()
    vc_d = nc.dram_tensor("vc_d", [8, 128, 1024], BF16, kind="Internal").ap()
    kg_d = nc.dram_tensor("kg_d", [2, 8, 128, 1024], BF16, kind="Internal").ap()
    vg_d = nc.dram_tensor("vg_d", [2, 8, 128, 1024], BF16, kind="Internal").ap()
    kvg_o = (nc.dram_tensor("kvg_o", [2, 16, 128, 1024], BF16, kind="ExternalOutput").ap()
             if dbgA else None)
    ctx_d = sg_d = out = wbig_d = mid_d = None
    if upto != "A":
        ctx_d = nc.dram_tensor("ctx_d", [32, 128, 1024], BF16,
                               kind="ExternalOutput" if dbgB else "Internal").ap()
    if upto in ("full", "C"):
        sg_d = nc.dram_tensor("sg_d", [32, 128, 1024], BF16,
                              kind="ExternalOutput" if dbgC else "Internal").ap()
        out = nc.dram_tensor("out", [1024, D], F32, kind="ExternalOutput").ap()
        if dbgC:
            wbig_d = nc.dram_tensor("wbig_d", [128, 1024], F32, kind="ExternalOutput").ap()
            mid_d = nc.dram_tensor("mid_d", [128, 1024], BF16, kind="ExternalOutput").ap()

    est = ExitStack()
    with est:
        sems = {}
        for name in ["scst", "sx0", "sx1", "sveps", "sstd", "sln", "stpg", "stpe",
                     "skvmm", "skev", "svev", "skd0", "skd1", "swkv0", "swkv1", "svtg",
                     "svte", "svd0", "svd1", "sbar", "scck", "sccv",
                     "swq0", "swq1", "sq", "sqr", "ssmm", "sexp", "smsk",
                     "sav", "scs", "srec", "sbc", "sctx", "sctd0", "sctd1", "sblk", "sblv", "sbld",
                     "sd0", "sd0dma", "swo0", "swo1", "sbw0", "sbw1", "swof", "ssC", "sfC",
                     "ssev", "ssil", "ssg", "ssgd0", "ssgd1", "sc0",
                     "ssgi0", "ssgi1", "st2", "soadd", "sout0", "sout1"]:
            sems[name] = est.enter_context(nc.semaphore(name))
        S = sems

        def gsb(name, shape, dt):
            return est.enter_context(nc.sbuf_tensor(name, shape, dt))

        crop_s = gsb("crop_s", [128, 1024], F32)
        srop_s = gsb("srop_s", [128, 1024], F32)
        m0_s = gsb("m0_s", [128, 128], BF16)
        m1_s = gsb("m1_s", [128, 128], BF16)
        identb_s = gsb("identb_s", [128, 128], BF16)
        identf_s = gsb("identf_s", [128, 128], F32)
        ones_s = gsb("ones_s", [128, 128], BF16)
        onesr_s = gsb("onesr_s", [128, 128], BF16)
        rep_s = gsb("rep_s", [8, 128], F32)
        cvec_s = gsb("cvec_s", [128, 4], F32)
        CB_EXP = cvec_s[:, 0:1]
        CB_ZERO = cvec_s[:, 1:2]
        CB_EPS = cvec_s[:, 3:4]
        nc.const_aps.register(F32, 0.0, cvec_s[:, 1:2])
        nc.const_aps.register(F32, 1.0, cvec_s[:, 2:3])
        xnfm = gsb("xnfm", [128, 32 * 1024], BF16)
        lb_s = gsb("lb_s", [128, 4096], BF16)
        mid_s = gsb("mid_s", [128, 1024], BF16)
        wbig_s = gsb("wbig_s", [128, 1024], F32)

        pb = [est.enter_context(nc.psum_tensor("pb%d" % i, [128, 512], F32))
              for i in range(8)]

        CONST_DMAS = 10

        stD = ExitStack()
        la_s = stD.enter_context(nc.sbuf_tensor("la_s", [128, 4096], BF16))
        rwt_s = stD.enter_context(nc.sbuf_tensor("rwt_s", [128, 256], BF16))
        lg_sb = stD.enter_context(nc.sbuf_tensor("lg_sb", [8, 1024], F32))
        lgt_sb = stD.enter_context(nc.sbuf_tensor("lgt_sb", [128, 64], F32))
        wdt_sb = stD.enter_context(nc.sbuf_tensor("wdt_sb", [128, 64], F32))
        wdf_sb = stD.enter_context(nc.sbuf_tensor("wdf_sb", [8, 1024], F32))
        srt8 = stD.enter_context(nc.sbuf_tensor("srt8", [128, 8], F32))
        m1bb = stD.enter_context(nc.sbuf_tensor("m1bb", [128, 8], F32))
        m2bb = stD.enter_context(nc.sbuf_tensor("m2bb", [128, 8], F32))
        d1b = stD.enter_context(nc.sbuf_tensor("d1b", [128, 8], F32))
        d2b = stD.enter_context(nc.sbuf_tensor("d2b", [128, 8], F32))
        m2nn = stD.enter_context(nc.sbuf_tensor("m2nn", [128, 8], F32))
        recs = stD.enter_context(nc.sbuf_tensor("recs", [128, 1], F32))
        m1s = stD.enter_context(nc.sbuf_tensor("m1s", [128, 1], F32))
        m2s = stD.enter_context(nc.sbuf_tensor("m2s", [128, 1], F32))
        dts = stD.enter_context(nc.sbuf_tensor("dts", [128, 1], F32))
        w1s = stD.enter_context(nc.sbuf_tensor("w1s", [128, 1], F32))
        w2s = stD.enter_context(nc.sbuf_tensor("w2s", [128, 1], F32))
        e1s = stD.enter_context(nc.sbuf_tensor("e1s", [128, 8], F32))
        e2s = stD.enter_context(nc.sbuf_tensor("e2s", [128, 8], F32))
        tts = stD.enter_context(nc.sbuf_tensor("tts", [128, 8], F32))

        stA = ExitStack()
        with stA:
            def asb(name, shape, dt):
                return stA.enter_context(nc.sbuf_tensor(name, shape, dt))
            x_in = [asb("x_in%d" % i, [128, D], F32) for i in range(2)]
            xn_tm = [asb("xn_tm%d" % i, [128, D], BF16) for i in range(2)]
            bno = asb("bno", [128, 48], F32)
            mvv = asb("mvv", [128, 2], F32)
            stdv = asb("stdv", [128, 1], F32)
            rstd = asb("rstd", [128, 1], F32)
            wkv_s = [asb("wkv_s%d" % i, [128, 4096], BF16) for i in range(2)]
            kst = [asb("kst%d" % i, [128, 1024], BF16) for i in range(2)]
            vfm = asb("vfm", [128, 8 * 1024], BF16)
            vst = [asb("vst%d" % i, [128, 1024], BF16) for i in range(2)]
            ra = asb("ra", [128, 512], F32)
            rb = asb("rb", [128, 512], F32)
            rs = asb("rs", [128, 512], F32)

            with nc.Block() as blk:
                @blk.sync
                def _(s):
                    for (dst, src) in [(crop_s, crop), (srop_s, srop),
                                       (m0_s, m0), (m1_s, m1),
                                       (identb_s, identb), (identf_s, identf),
                                       (ones_s, ones128), (onesr_s, onesrow),
                                       (rep_s, rep), (cvec_s, cvec)]:
                        s.dma_start(dst[:, :], src).then_inc(S["scst"], 16)
                    for j in range(8):
                        if j >= 2:
                            s.wait_ge(S["sln"], j - 1)
                        s.dma_start(x_in[j % 2][:, :], xo[128 * j:128 * j + 128, :]
                                    ).then_inc(S["sx%d" % (j % 2)], 16)
                    for g in range(8):
                        s.wait_ge(S["skev"], g + 1)
                        s.dma_start(kc_d[g], kst[g % 2][:, :]
                                    ).then_inc(S["skd%d" % (g % 2)], 16)
                    for j in range(8):
                        s.wait_ge(S["svte"], j + 1)
                        s.dma_start(vc_d[j], vst[j % 2][:, :]
                                    ).then_inc(S["svd%d" % (j % 2)], 16)

                @blk.gpsimd
                def _(g_):
                    for goc in range(16):
                        if goc >= 2:
                            g_.wait_ge(S["skvmm"], 2 * (goc - 1))
                        src4 = wk[goc] if goc < 8 else wv[goc - 8]
                        src = bass.AP(src4.tensor, src4.offset,
                                      [[src4.ap[0][0], 128], [1, 4096]])
                        g_.dma_start(wkv_s[goc % 2][:, :], src,
                                     max_dma_last_dim=8192).then_inc(S["swkv%d" % (goc % 2)], 16)
                    g_.dma_start(rwt_s[:, :],
                                 bass.AP(rwt.tensor, rwt.offset, [[rwt.ap[0][0], 128], [1, 256]]),
                                 max_dma_last_dim=8192).then_inc(S["sd0dma"], 16)
                    g_.dma_start(la_s[:, :],
                                 bass.AP(la.tensor, la.offset, [[la.ap[0][0], 128], [1, 4096]]),
                                 max_dma_last_dim=8192).then_inc(S["sd0dma"], 16)
                    g_.dma_start(lb_s[:, :],
                                 bass.AP(lb.tensor, lb.offset, [[lb.ap[0][0], 128], [1, 4096]]),
                                 max_dma_last_dim=8192).then_inc(S["sd0dma"], 16)
                    g_.wait_ge(S["skd0"], 64)
                    g_.wait_ge(S["skd1"], 64)
                    g_.collective_compute(
                        "AllGather", mybir.AluOpType.bypass,
                        replica_groups=[[0, 1], [2, 3], [4, 5], [6, 7]],
                        ins=[kc_d.opt()], outs=[kg_d.opt()],
                    ).then_inc(S["scck"], 1)
                    g_.wait_ge(S["svd0"], 64)
                    g_.wait_ge(S["svd1"], 64)
                    g_.collective_compute(
                        "AllGather", mybir.AluOpType.bypass,
                        replica_groups=[[0, 1], [2, 3], [4, 5], [6, 7]],
                        ins=[vc_d.opt()], outs=[vg_d.opt()],
                    ).then_inc(S["sccv"], 1)

                @blk.vector
                def _(v):
                    v.wait_ge(S["scst"], 16 * CONST_DMAS)

                    def ln(j):
                        v.wait_ge(S["sx%d" % (j % 2)], 16 * (j // 2 + 1))
                        xs = x_in[j % 2]
                        for u in range(8):
                            v.bn_stats(bno[:, 6 * u:6 * u + 6], xs[:, 512 * u:512 * u + 512])
                        v.drain()
                        v.bn_aggr(mvv[:, :], bno[:, :])
                        v.drain().then_inc(S["sveps"], 1)
                        v.wait_ge(S["sstd"], j + 1)
                        v.reciprocal(rstd[:, :], stdv[:, :])
                        v.drain()
                        if j >= 2:
                            v.wait_ge(S["stpg"], 8 * (j - 1))
                        v.tensor_scalar(xn_tm[j % 2][:, :], xs[:, :], mvv[:, 0:1],
                                        rstd[:, 0:1], Alu.subtract, Alu.mult
                                        ).then_inc(S["sln"], 1)

                    def ropeK(g):
                        pair = (pb[2], pb[3]) if g % 2 == 0 else (pb[4], pb[5])
                        if g >= 2:
                            v.wait_ge(S["skd%d" % (g % 2)], 16 * (g // 2))
                        for th in range(2):
                            v.wait_ge(S["skvmm"], 2 * g + th + 1)
                            ps = pair[th]
                            v.tensor_tensor(ra[:, :], ps[:, :],
                                            crop_s[:, 512 * th:512 * th + 512], Alu.mult)
                            v.tensor_copy(rs[0:64, :], ps[64:128, :])
                            v.tensor_copy(rs[64:128, :], ps[0:64, :])
                            v.tensor_tensor(rb[:, :], rs[:, :],
                                            srop_s[:, 512 * th:512 * th + 512], Alu.mult)
                            v.tensor_tensor(kst[g % 2][:, 512 * th:512 * th + 512],
                                            ra[:, :], rb[:, :], Alu.add)
                        v.drain().then_inc(S["skev"], 1)

                    def vstev(j):
                        v.wait_ge(S["svtg"], j + 1)
                        if j >= 2:
                            v.wait_ge(S["svd%d" % (j % 2)], 16 * (j // 2))
                        v.tensor_copy(vst[j % 2][:, 0:512], pb[6][:, :])
                        v.tensor_copy(vst[j % 2][:, 512:1024], pb[7][:, :])
                        v.drain().then_inc(S["svte"], 1)

                    for j in range(8):
                        ln(j)
                    for g in range(8):
                        ropeK(g)
                    for j in range(8):
                        vstev(j)

                @blk.scalar
                def _(a):
                    a.wait_ge(S["scst"], 16 * CONST_DMAS)

                    def sqrt_(j):
                        a.wait_ge(S["sveps"], j + 1)
                        a.activation(stdv[:, :], mvv[:, 1:2], AF.Sqrt,
                                     bias=CB_EPS, scale=1.0)
                        a.drain().then_inc(S["sstd"], 1)

                    def tevac_sub(j):
                        for grp in range(8):
                            gc = 8 * j + grp
                            a.wait_ge(S["stpg"], gc + 1)
                            dst = sview(xnfm, (4 * grp) * 1024 + 128 * j,
                                        [[1024, 4], [1, 128]])
                            a.copy(dst, pb[gc % 2][:, :]).then_inc(S["stpe"], 1)

                    sqrt_(0)
                    sqrt_(1)
                    for j in range(8):
                        tevac_sub(j)
                        if j + 2 <= 7:
                            sqrt_(j + 2)
                    for g in range(8):
                        pair = (pb[2], pb[3]) if g % 2 == 0 else (pb[4], pb[5])
                        for th in range(2):
                            a.wait_ge(S["skvmm"], 2 * (8 + g) + th + 1)
                            inst = a.copy(vfm[:, g * 1024 + 512 * th:g * 1024 + 512 * th + 512],
                                          pair[th][:, :])
                            if th == 1:
                                inst.then_inc(S["svev"], 1)

                @blk.tensor
                def _(t):
                    t.wait_ge(S["scst"], 16 * CONST_DMAS)
                    for j in range(8):
                        t.wait_ge(S["sln"], j + 1)
                        for grp in range(8):
                            gc = 8 * j + grp
                            if gc >= 2:
                                t.wait_ge(S["stpe"], gc - 1)
                            unit = pb[gc % 2]
                            inst = None
                            for u in range(4):
                                kf = 4 * grp + u
                                inst = t.matmul(unit[:, 128 * u:128 * u + 128],
                                                xn_tm[j % 2][:, 128 * kf:128 * kf + 128],
                                                identb_s[:, :], start=True, stop=True)
                            inst.then_inc(S["stpg"], 1)
                    t.wait_ge(S["stpe"], 64)
                    for goc in range(16):
                        t.wait_ge(S["swkv%d" % (goc % 2)], 16 * (goc // 2 + 1))
                        pgoc = goc - 2
                        if pgoc >= 0:
                            if pgoc < 8:
                                t.wait_ge(S["skev"], pgoc + 1)
                            else:
                                t.wait_ge(S["svev"], pgoc - 8 + 1)
                        pair = (pb[2], pb[3]) if goc % 2 == 0 else (pb[4], pb[5])
                        for th in range(2):
                            inst = None
                            for kf in range(32):
                                inst = t.matmul(
                                    pair[th][:, :],
                                    wkv_s[goc % 2][:, 128 * kf:128 * kf + 128],
                                    xnfm[:, 1024 * kf + 512 * th:1024 * kf + 512 * th + 512],
                                    start=(kf == 0), stop=(kf == 31))
                            inst.then_inc(S["skvmm"], 1)
                    t.wait_ge(S["svev"], 8)
                    for j in range(8):
                        if j >= 1:
                            t.wait_ge(S["svte"], j)
                        inst = None
                        for g in range(8):
                            unit = pb[6] if g < 4 else pb[7]
                            inst = t.matmul(unit[:, 128 * (g % 4):128 * (g % 4) + 128],
                                            vfm[:, g * 1024 + 128 * j:g * 1024 + 128 * j + 128],
                                            identb_s[:, :], start=True, stop=True)
                        inst.then_inc(S["svtg"], 1)

                @blk.sync
                def _(s):
                    s.nop().then_inc(S["sbar"], 1)

                @blk.gpsimd
                def _(g_):
                    g_.nop().then_inc(S["sbar"], 1)

                @blk.vector
                def _(v):
                    v.nop().then_inc(S["sbar"], 1)

                @blk.scalar
                def _(a):
                    a.nop().then_inc(S["sbar"], 1)

                @blk.tensor
                def _(t):
                    t.nop().then_inc(S["sbar"], 1)

        if upto == "A":
            with nc.Block() as blk:
                @blk.sync
                def _(s):
                    s.wait_ge(S["sbar"], 5)
                    s.wait_ge(S["scck"], 1)
                    s.wait_ge(S["sccv"], 1)
                    s.dma_start(kvg_o[:, 0:8], kg_d).then_inc(S["sbld"], 16)
                    s.dma_start(kvg_o[:, 8:16], vg_d).then_inc(S["sbld"], 16)
                    s.wait_ge(S["sbld"], 32)
            stD.close()
            return nc

        d0 = [0]

        def d0w(eng):
            eng.wait_ge(S["sd0"], d0[0])

        def d0i(eng):
            d0[0] += 1
            eng.drain().then_inc(S["sd0"], 1)

        with nc.Block() as blk:
            @blk.tensor
            def _(t):
                t.wait_ge(S["sbar"], 5)
                t.wait_ge(S["sd0dma"], 48)
                t.drain()
                for th in range(2):
                    inst = None
                    for kf in range(32):
                        inst = t.matmul(pb[th][0:8, :], rwt_s[:, 8 * kf:8 * kf + 8],
                                        xnfm[:, 1024 * kf + 512 * th:1024 * kf + 512 * th + 512],
                                        start=(kf == 0), stop=(kf == 31))
                    d0i(t)

            @blk.vector
            def _(v):
                v.wait_ge(S["sbar"], 5)
                d0w(v)
                v.tensor_copy(lg_sb[0:8, 0:512], pb[0][0:8, :])
                v.tensor_copy(lg_sb[0:8, 512:1024], pb[1][0:8, :])
                d0i(v)

            @blk.tensor
            def _(t):
                d0w(t)
                for i in range(8):
                    t.matmul(pb[2][:, 8 * i:8 * i + 8],
                             lg_sb[0:8, 128 * i:128 * i + 128],
                             identf_s[0:8, 0:8], start=True, stop=True)
                d0i(t)

            @blk.vector
            def _(v):
                d0w(v)
                v.tensor_copy(lgt_sb[:, :], pb[2][:, 0:64])
                d0i(v)

            for tb in range(8):
                @blk.vector
                def _(v, tb=tb):
                    d0w(v)
                    blkl = lgt_sb[:, 8 * tb:8 * tb + 8]
                    v.max(out=srt8[:, :], in_=blkl)
                    v.drain()
                    v.tensor_scalar(m1bb[:, :], blkl, 0.0, srt8[:, 0:1], Alu.mult, Alu.add)
                    v.tensor_scalar(m2bb[:, :], blkl, 0.0, srt8[:, 1:2], Alu.mult, Alu.add)
                    v.drain()
                    v.tensor_tensor(d1b[:, :], blkl, m1bb[:, :], Alu.subtract)
                    v.tensor_tensor(d2b[:, :], blkl, m2bb[:, :], Alu.subtract)
                    v.tensor_tensor(m2nn[:, :], m2bb[:, :], blkl, Alu.subtract)
                    v.drain()
                    v.tensor_scalar(d1b[:, :], d1b[:, :], 1e6, -100.0, Alu.mult, Alu.max)
                    v.tensor_tensor(d2b[:, :], d2b[:, :], m2nn[:, :], Alu.min)
                    v.drain()
                    v.tensor_scalar(d2b[:, :], d2b[:, :], 1e6, -100.0, Alu.mult, Alu.max)
                    v.tensor_tensor(dts[:, :], srt8[:, 1:2], srt8[:, 0:1], Alu.subtract)
                    d0i(v)

                @blk.scalar
                def _(a, tb=tb):
                    d0w(a)
                    a.activation(e1s[:, :], d1b[:, :], AF.Exp, bias=CB_ZERO, scale=1.0)
                    a.activation(e2s[:, :], d2b[:, :], AF.Exp, bias=CB_ZERO, scale=1.0)
                    a.activation(m1s[:, :], dts[:, :], AF.Exp, bias=CB_ZERO, scale=1.0)
                    d0i(a)

                @blk.vector
                def _(v, tb=tb):
                    d0w(v)
                    v.tensor_scalar(m2s[:, :], m1s[:, :], 1.0, None, Alu.add)
                    v.drain()
                    v.reciprocal(recs[:, :], m2s[:, :])
                    v.drain()
                    v.tensor_tensor(w2s[:, :], m1s[:, :], recs[:, :], Alu.mult)
                    v.drain()
                    v.tensor_scalar(w1s[:, :], w2s[:, :], -1.0, 1.0, Alu.mult, Alu.add)
                    v.drain()
                    v.tensor_scalar(tts[:, :], e1s[:, :], w1s[:, 0:1], None, Alu.mult)
                    v.drain()
                    v.scalar_tensor_tensor(wdt_sb[:, 8 * tb:8 * tb + 8], e2s[:, :],
                                           w2s[:, 0:1], tts[:, :], Alu.mult, Alu.add)
                    d0i(v)

            @blk.tensor
            def _(t):
                d0w(t)
                for i in range(8):
                    unit = pb[3] if i < 4 else pb[6]
                    t.matmul(unit[0:8, 128 * (i % 4):128 * (i % 4) + 128],
                             wdt_sb[:, 8 * i:8 * i + 8], identf_s[:, :],
                             start=True, stop=True)
                d0i(t)

            @blk.vector
            def _(v):
                d0w(v)
                for i in range(4):
                    v.tensor_copy(wdf_sb[0:8, 128 * i:128 * i + 128],
                                  pb[3][0:8, 128 * i:128 * i + 128])
                for i in range(4):
                    v.tensor_copy(wdf_sb[0:8, 512 + 128 * i:512 + 128 * i + 128],
                                  pb[6][0:8, 128 * i:128 * i + 128])
                d0i(v)
            D0_WDF = d0[0]

            @blk.tensor
            def _(t):
                d0w(t)
                for th in range(2):
                    t.matmul(pb[4 + th][:, :], rep_s[:, :],
                             wdf_sb[0:8, 512 * th:512 * th + 512],
                             start=True, stop=True)
                d0i(t)

            @blk.vector
            def _(v):
                d0w(v)
                v.tensor_copy(wbig_s[:, 0:512], pb[4][:, :])
                v.tensor_copy(wbig_s[:, 512:1024], pb[5][:, :])
                d0i(v)

            @blk.tensor
            def _(t):
                d0w(t)
                for th in range(2):
                    inst = None
                    for kf in range(32):
                        inst = t.matmul(pb[2 + th][:, :], la_s[:, 128 * kf:128 * kf + 128],
                                        xnfm[:, 1024 * kf + 512 * th:1024 * kf + 512 * th + 512],
                                        start=(kf == 0), stop=(kf == 31))
                    d0i(t)

            @blk.vector
            def _(v):
                d0w(v)
                v.tensor_tensor(mid_s[:, 0:512], pb[2][:, :], wbig_s[:, 0:512], Alu.mult)
                v.tensor_tensor(mid_s[:, 512:1024], pb[3][:, :],
                                wbig_s[:, 512:1024], Alu.mult)
                d0i(v)

        D0_FINAL = d0[0]
        stD.close()

        stB = ExitStack()
        with stB:
            def bsb(name, shape, dt):
                return stB.enter_context(nc.sbuf_tensor(name, shape, dt))
            k_sb = bsb("k_sb", [128, 8 * 2048], BF16)
            v_sb = bsb("v_sb", [128, 16 * 1024], BF16)
            wq_s = [bsb("wq_s%d" % i, [128, 4096], BF16) for i in range(2)]
            q_sb = bsb("q_sb", [128, 1024], BF16)
            p_sb = [bsb("p_sb%d" % i, [128, 1024], BF16) for i in range(2)]
            ra2 = bsb("ra2", [128, 512], F32)
            rb2 = bsb("rb2", [128, 512], F32)
            rs2 = bsb("rs2", [128, 512], F32)
            rz = bsb("rz", [128, 1024], BF16)
            cte = bsb("cte", [128, 1024], F32)
            cxs = [bsb("cxs%d" % i, [128, 1024], BF16) for i in range(2)]

            with nc.Block() as blk:
                @blk.gpsimd
                def _(g_):
                    for h in range(32):
                        if h >= 2:
                            g_.wait_ge(S["sq"], 2 * (h - 1))
                        src = bass.AP(wq[h].tensor, wq[h].offset,
                                      [[wq[h].ap[0][0], 128], [1, 4096]])
                        g_.dma_start(wq_s[h % 2][:, :], src,
                                     max_dma_last_dim=8192).then_inc(S["swq%d" % (h % 2)], 16)

                @blk.sync
                def _(s):
                    s.wait_ge(S["sbar"], 5)
                    s.wait_ge(S["scck"], 1)
                    for r in range(2):
                        for g_ in range(8):
                            src = bass.AP(kg_d.tensor,
                                          kg_d.offset + (8 * r + g_) * (128 * 1024),
                                          [[1024, 128], [128, 8], [1, 128]])
                            dst = sview(k_sb, 2048 * g_ + 128 * r, [[256, 8], [1, 128]])
                            s.dma_start(dst, src).then_inc(S["sblk"], 16)
                    s.wait_ge(S["sccv"], 1)
                    for r in range(2):
                        src = bass.AP(vg_d.tensor,
                                      vg_d.offset + 8 * r * (128 * 1024),
                                      [[1024, 128], [131072, 8], [1, 1024]])
                        dst = sview(v_sb, 1024 * r, [[2048, 8], [1, 1024]])
                        s.dma_start(dst, src).then_inc(S["sblv"], 16)
                    for h in range(32):
                        s.wait_ge(S["sctx"], h + 1)
                        s.dma_start(ctx_d[h], cxs[h % 2][:, :]).then_inc(S["sctd%d" % (h % 2)], 16)

                SUNIT = [0, 1, 6]

                @blk.tensor
                def _(t):
                    t.wait_ge(S["sd0"], D0_WDF)
                    for h in range(32):
                        gq = h // 4
                        if h == 0:
                            t.wait_ge(S["swq0"], 16)
                            t.wait_ge(S["sqr"], 0)
                        for th in range(2):
                            inst = None
                            for kf in range(32):
                                inst = t.matmul(
                                    pb[6 + th][:, :],
                                    wq_s[h % 2][:, 128 * kf:128 * kf + 128],
                                    xnfm[:, 1024 * kf + 512 * th:1024 * kf + 512 * th + 512],
                                    start=(kf == 0), stop=(kf == 31))
                            inst.then_inc(S["sq"], 1)
                        if h == 0:
                            t.wait_ge(S["sblk"], 16 * 16)
                        t.wait_ge(S["sqr"], h + 1)
                        started_r = set()
                        started_c = set()

                        def s_mm(k):
                            g_, off, n, first = CHUNKS[k]
                            ci = NCH * h + k
                            if ci >= 3:
                                t.wait_ge(S["sexp"], ci - 2)
                            t.matmul(pb[SUNIT[ci % 3]][:, 0:n],
                                     k_sb[:, 2048 * gq + 128 * g_:2048 * gq + 128 * g_ + 128],
                                     q_sb[:, off:off + n], start=True, stop=True
                                     ).then_inc(S["ssmm"], 1)

                        def cs_av(k):
                            g_, off, n, first = CHUNKS[k]
                            ci = NCH * h + k
                            if ci == 0:
                                t.wait_ge(S["sd0"], D0_FINAL)
                            if first:
                                t.wait_ge(S["smsk"], 16 * h + g_ + 1)
                            else:
                                t.wait_ge(S["sexp"], ci + 1)
                            u = off // 512
                            o2 = off % 512
                            inst = t.matmul(pb[4 + u][:, o2:o2 + n], ones_s[:, :],
                                            p_sb[g_ % 2][:, off:off + n],
                                            start=(u not in started_r),
                                            stop=(k == LAST_K[u]))
                            started_r.add(u)
                            if k == NCH - 1:
                                inst.then_inc(S["scs"], 1)
                            inst = t.matmul(
                                pb[2 + u][:, o2:o2 + n],
                                v_sb[:, 1024 * g_ + 128 * gq:1024 * g_ + 128 * gq + 128],
                                p_sb[g_ % 2][:, off:off + n],
                                start=(u not in started_c),
                                stop=(k == LAST_K[u]))
                            started_c.add(u)
                            inst.then_inc(S["sav"], 1)

                        s_mm(0)
                        s_mm(1)
                        if h == 0:
                            t.wait_ge(S["sblv"], 32)
                        for k in range(2, NCH):
                            s_mm(k)
                            cs_av(k - 2)
                        cs_av(NCH - 2)
                        cs_av(NCH - 1)
                        t.wait_ge(S["srec"], h + 1)
                        inst = None
                        for u in range(2):
                            inst = t.matmul(pb[6 + u][:, :], onesr_s[:, :],
                                            rz[:, 512 * u:512 * u + 512],
                                            start=True, stop=True)
                        inst.then_inc(S["sbc"], 1)
                        if h + 1 < 32:
                            t.wait_ge(S["swq%d" % ((h + 1) % 2)], 16 * ((h + 1) // 2 + 1))
                            t.wait_ge(S["sctx"], h + 1)
                            t.wait_ge(S["srec"], h + 1)

                @blk.scalar
                def _(a):
                    for h in range(32):
                        for k in range(NCH):
                            g_, off, n, first = CHUNKS[k]
                            ci = NCH * h + k
                            a.wait_ge(S["ssmm"], ci + 1)
                            if g_ >= 2:
                                a.wait_ge(S["sav"], NCH * h + CH_OF_G[g_ - 2][-1] + 1)
                            elif h >= 1:
                                a.wait_ge(S["sav"], NCH * (h - 1) + CH_OF_G[g_ + 14][-1] + 1)
                            a.activation(p_sb[g_ % 2][:, off:off + n],
                                         pb[[0, 1, 6][ci % 3]][:, 0:n],
                                         AF.Exp, bias=CB_EXP, scale=SCALE
                                         ).then_inc(S["sexp"], 1)

                @blk.vector
                def _(v):
                    v.memset(rz[:, :], 0.0)
                    for h in range(32):
                        for th in range(2):
                            v.wait_ge(S["sq"], 2 * h + th + 1)
                            ps = pb[6 + th]
                            v.tensor_tensor(ra2[:, :], ps[:, :],
                                            crop_s[:, 512 * th:512 * th + 512], Alu.mult)
                            v.tensor_copy(rs2[0:64, :], ps[64:128, :])
                            v.tensor_copy(rs2[64:128, :], ps[0:64, :])
                            v.tensor_tensor(rb2[:, :], rs2[:, :],
                                            srop_s[:, 512 * th:512 * th + 512], Alu.mult)
                            v.tensor_tensor(q_sb[:, 512 * th:512 * th + 512],
                                            ra2[:, :], rb2[:, :], Alu.add)
                        v.drain().then_inc(S["sqr"], 1)
                        for g_ in range(16):
                            kf_ = CH_OF_G[g_][0]
                            off = CHUNKS[kf_][1]
                            v.wait_ge(S["sexp"], NCH * h + kf_ + 1)
                            mt = m0_s if g_ % 2 == 0 else m1_s
                            v.tensor_tensor(p_sb[g_ % 2][:, off:off + 128],
                                            p_sb[g_ % 2][:, off:off + 128],
                                            mt[:, :], Alu.mult)
                            v.drain().then_inc(S["smsk"], 1)
                        v.wait_ge(S["scs"], h + 1)
                        if h >= 1:
                            v.wait_ge(S["sbc"], h)
                        with nc.allow_low_precision(reason="1/r bcast in bf16: per-column scale, rel 2^-9"):
                            v.reciprocal(rz[0:1, 0:512], pb[4][0:1, :])
                            v.reciprocal(rz[0:1, 512:1024], pb[5][0:1, :])
                        v.drain().then_inc(S["srec"], 1)
                        v.wait_ge(S["sav"], NCH * (h + 1))
                        v.wait_ge(S["sbc"], h + 1)
                        if h >= 2:
                            v.wait_ge(S["sctd%d" % (h % 2)], 16 * (h // 2))
                        v.tensor_copy(cte[:, 0:512], pb[2][:, :])
                        v.tensor_copy(cte[:, 512:1024], pb[3][:, :])
                        v.tensor_tensor(cxs[h % 2][:, 0:512], cte[:, 0:512],
                                        pb[6][:, :], Alu.mult)
                        v.tensor_tensor(cxs[h % 2][:, 512:1024], cte[:, 512:1024],
                                        pb[7][:, :], Alu.mult)
                        v.drain().then_inc(S["sctx"], 1)

                @blk.sync
                def _(s):
                    s.nop().then_inc(S["sbar"], 1)

                @blk.gpsimd
                def _(g_):
                    g_.nop().then_inc(S["sbar"], 1)

                @blk.vector
                def _(v):
                    v.nop().then_inc(S["sbar"], 1)

                @blk.scalar
                def _(a):
                    a.nop().then_inc(S["sbar"], 1)

                @blk.tensor
                def _(t):
                    t.nop().then_inc(S["sbar"], 1)

        if upto == "B":
            with nc.Block() as blk:
                @blk.sync
                def _(s):
                    s.wait_ge(S["sbar"], 10)
                    s.wait_ge(S["sctd0"], 16 * 16)
                    s.wait_ge(S["sctd1"], 16 * 16)
            return nc

        stC = ExitStack()
        with stC:
            def csb(name, shape, dt):
                return stC.enter_context(nc.sbuf_tensor(name, shape, dt))
            ctx_s = csb("ctx_s", [128, 32 * 1024], BF16)
            wo_s = [csb("wo_s%d" % i, [128, 4096], BF16) for i in range(2)]
            bwt_s = [csb("bwt_s%d" % i, [128, 4096], BF16) for i in range(2)]
            s_sb = csb("s_sb", [128, 1024], BF16)
            g_sb = [csb("g_sb%d" % i, [128, 1024], BF16) for i in range(2)]
            sg_sb = [csb("sg_sb%d" % i, [128, 1024], BF16) for i in range(2)]

            with nc.Block() as blk:
                @blk.gpsimd
                def _(g_):
                    for oc in range(32):
                        if oc >= 2:
                            g_.wait_ge(S["swof"], oc - 1)
                        g_.dma_start(wo_s[oc % 2][:, :],
                                     bass.AP(wo[oc].tensor, wo[oc].offset,
                                             [[wo[oc].ap[0][0], 128], [1, 4096]]),
                                     max_dma_last_dim=8192).then_inc(S["swo%d" % (oc % 2)], 16)
                        g_.dma_start(bwt_s[oc % 2][:, :],
                                     bass.AP(bwt[oc].tensor, bwt[oc].offset,
                                             [[bwt[oc].ap[0][0], 128], [1, 4096]]),
                                     max_dma_last_dim=8192).then_inc(S["sbw%d" % (oc % 2)], 16)

                @blk.sync
                def _(s):
                    s.wait_ge(S["sbar"], 10)
                    s.wait_ge(S["sctd0"], 16 * 16)
                    s.wait_ge(S["sctd1"], 16 * 16)
                    s.dma_start(ctx_s[:, :].rearrange("p (a c) -> p a c", a=32),
                                ctx_d.rearrange("a p c -> p a c")).then_inc(S["sc0"], 16)
                    if dbgC:
                        s.dma_start(wbig_d, wbig_s[:, :]).then_inc(S["sc0"], 16)
                        s.dma_start(mid_d, mid_s[:, :]).then_inc(S["sc0"], 16)
                    for oc in range(32):
                        s.wait_ge(S["ssg"], oc + 1)
                        s.dma_start(sg_d[oc], sg_sb[oc % 2][:, :]).then_inc(S["ssgd%d" % (oc % 2)], 16)

                @blk.tensor
                def _(t):
                    t.wait_ge(S["sbar"], 10)
                    t.wait_ge(S["sc0"], 16)
                    for oc in range(32):
                        t.wait_ge(S["swo%d" % (oc % 2)], 16 * (oc // 2 + 1))
                        t.wait_ge(S["sbw%d" % (oc % 2)], 16 * (oc // 2 + 1))
                        if oc >= 2:
                            t.wait_ge(S["ssev"], oc - 1)
                            t.wait_ge(S["ssil"], 2 * (oc - 1))
                        spair = (pb[0], pb[1]) if oc % 2 == 0 else (pb[4], pb[5])
                        fpair = (pb[2], pb[3]) if oc % 2 == 0 else (pb[6], pb[7])
                        for th in range(2):
                            inst = None
                            for kf in range(32):
                                inst = t.matmul(
                                    spair[th][:, :], wo_s[oc % 2][:, 128 * kf:128 * kf + 128],
                                    ctx_s[:, 1024 * kf + 512 * th:1024 * kf + 512 * th + 512],
                                    start=(kf == 0), stop=(kf == 31))
                            inst.then_inc(S["ssC"], 1)
                        for th in range(2):
                            for kf in range(32):
                                t.matmul(
                                    fpair[th][:, :], bwt_s[oc % 2][:, 128 * kf:128 * kf + 128],
                                    xnfm[:, 1024 * kf + 512 * th:1024 * kf + 512 * th + 512],
                                    start=(kf == 0), stop=False)
                            t.matmul(fpair[th][:, :], lb_s[:, 128 * oc:128 * oc + 128],
                                     mid_s[:, 512 * th:512 * th + 512],
                                     start=False, stop=True).then_inc(S["sfC"], 1)
                        t.nop().then_inc(S["swof"], 1)

                @blk.scalar
                def _(a):
                    for oc in range(32):
                        fpair = (pb[2], pb[3]) if oc % 2 == 0 else (pb[6], pb[7])
                        if oc >= 2:
                            a.wait_ge(S["ssg"], oc - 1)
                        for th in range(2):
                            a.wait_ge(S["sfC"], 2 * oc + th + 1)
                            a.activation(g_sb[oc % 2][:, 512 * th:512 * th + 512],
                                         fpair[th][:, :], AF.Silu, bias=CB_ZERO,
                                         scale=1.0).then_inc(S["ssil"], 1)

                @blk.vector
                def _(v):
                    for oc in range(32):
                        spair = (pb[0], pb[1]) if oc % 2 == 0 else (pb[4], pb[5])
                        v.wait_ge(S["ssC"], 2 * oc + 2)
                        v.tensor_copy(s_sb[:, 0:512], spair[0][:, :])
                        v.tensor_copy(s_sb[:, 512:1024], spair[1][:, :])
                        v.drain().then_inc(S["ssev"], 1)
                        v.wait_ge(S["ssil"], 2 * oc + 2)
                        if oc >= 2:
                            v.wait_ge(S["ssgd%d" % (oc % 2)], 16 * (oc // 2))
                        v.tensor_tensor(sg_sb[oc % 2][:, :], s_sb[:, :], g_sb[oc % 2][:, :],
                                        Alu.mult)
                        v.drain().then_inc(S["ssg"], 1)

                @blk.sync
                def _(s):
                    s.nop().then_inc(S["sbar"], 1)

                @blk.gpsimd
                def _(g_):
                    g_.nop().then_inc(S["sbar"], 1)

                @blk.vector
                def _(v):
                    v.nop().then_inc(S["sbar"], 1)

                @blk.scalar
                def _(a):
                    a.nop().then_inc(S["sbar"], 1)

                @blk.tensor
                def _(t):
                    t.nop().then_inc(S["sbar"], 1)

        stE = ExitStack()
        with stE:
            def esb(name, shape, dt):
                return stE.enter_context(nc.sbuf_tensor(name, shape, dt))
            sgi = [esb("sgi%d" % i, [128, 4096], BF16) for i in range(2)]
            xr = [esb("xr%d" % i, [128, 4096], F32) for i in range(2)]
            ob = [esb("ob%d" % i, [128, 4096], F32) for i in range(2)]

            with nc.Block() as blk:
                @blk.sync
                def _(s):
                    s.wait_ge(S["sbar"], 15)
                    s.wait_ge(S["ssgd0"], 16 * 16)
                    s.wait_ge(S["ssgd1"], 16 * 16)

                    def out_dma(tb):
                        s.wait_ge(S["soadd"], 8 * (tb + 1))
                        s.dma_start(out[128 * tb:128 * tb + 128, :], ob[tb % 2][:, :]
                                    ).then_inc(S["sout%d" % (tb % 2)], 16)

                    for tb in range(8):
                        if tb >= 2:
                            s.wait_ge(S["st2"], 8 * (tb - 1))
                        for half in range(2):
                            src = sg_d[16 * half:16 * half + 16, :, 128 * tb:128 * tb + 128]
                            s.dma_start(
                                sgi[tb % 2][:, 2048 * half:2048 * half + 2048].rearrange(
                                    "p (a c) -> p a c", a=16),
                                src.rearrange("a p c -> p a c")).then_inc(S["ssgi%d" % (tb % 2)], 16)
                        if tb >= 2:
                            s.wait_ge(S["soadd"], 8 * (tb - 1))
                        s.dma_start(xr[tb % 2][:, :], xo[128 * tb:128 * tb + 128, :]
                                    ).then_inc(S["ssgi%d" % (tb % 2)], 16)
                        if tb >= 2:
                            out_dma(tb - 2)
                    out_dma(6)
                    out_dma(7)
                    s.wait_ge(S["sout0"], 16 * 4)
                    s.wait_ge(S["sout1"], 16 * 4)

                @blk.tensor
                def _(t):
                    t.wait_ge(S["sbar"], 15)
                    for tb in range(8):
                        t.wait_ge(S["ssgi%d" % (tb % 2)], 48 * (tb // 2 + 1))
                        for grp in range(8):
                            if tb >= 1:
                                t.wait_ge(S["soadd"], 8 * (tb - 1) + grp + 1)
                            unit = pb[grp]
                            inst = None
                            for u in range(4):
                                oc = 4 * grp + u
                                inst = t.matmul(unit[:, 128 * u:128 * u + 128],
                                                sgi[tb % 2][:, 128 * oc:128 * oc + 128],
                                                identb_s[:, :], start=True, stop=True)
                            inst.then_inc(S["st2"], 1)

                @blk.vector
                def _(v):
                    for tb in range(8):
                        for grp in range(8):
                            cum = 8 * tb + grp
                            v.wait_ge(S["st2"], cum + 1)
                            if tb >= 2 and grp == 0:
                                v.wait_ge(S["sout%d" % (tb % 2)], 16 * (tb // 2))
                            v.tensor_tensor(ob[tb % 2][:, 512 * grp:512 * grp + 512],
                                            xr[tb % 2][:, 512 * grp:512 * grp + 512],
                                            pb[grp][:, :], Alu.add).then_inc(S["soadd"], 1)

    return nc


_NC_CACHE = {}


def _get_nc(upto="full"):
    if upto not in _NC_CACHE:
        _NC_CACHE[upto] = build_nc(upto)
    return _NC_CACHE[upto]


def make_inputs(L0_attn_x, L0_attn_Wq, L0_attn_Wk, L0_attn_Wv, L0_attn_Wo,
                L0_ffn_base_W, L0_ffn_router_W, L0_ffn_lora_A, L0_ffn_lora_B):
    f32 = np.float32
    wq_r = np.ascontiguousarray(np.asarray(L0_attn_Wq, f32).reshape(32, 128, 32, 128).transpose(2, 1, 0, 3))
    wk_r = np.ascontiguousarray(np.asarray(L0_attn_Wk, f32).reshape(32, 128, 8, 128).transpose(2, 1, 0, 3))
    wv_r = np.ascontiguousarray(np.asarray(L0_attn_Wv, f32).reshape(32, 128, 8, 128).transpose(2, 1, 0, 3))
    wo_r = np.ascontiguousarray(np.asarray(L0_attn_Wo, f32).reshape(32, 128, 32, 128).transpose(2, 1, 0, 3))
    bwt_r = np.ascontiguousarray(np.asarray(L0_ffn_base_W, f32).T.reshape(32, 128, 32, 128).transpose(2, 1, 0, 3))
    la_r = np.ascontiguousarray(np.asarray(L0_ffn_lora_A, f32).transpose(1, 0, 2).reshape(32, 128, 128).transpose(1, 0, 2))
    lb_r = np.ascontiguousarray(np.asarray(L0_ffn_lora_B, f32).reshape(128, 32, 128))
    rwt_r = np.ascontiguousarray(np.asarray(L0_ffn_router_W, f32).T.reshape(32, 128, 8).transpose(1, 0, 2))

    tri = np.tril(np.ones((128, 128), np.float32)).T
    identb_np = np.eye(128, dtype=BFNP)
    identf_np = np.eye(128, dtype=np.float32)
    ones128_np = np.ones((128, 128), BFNP)
    onesrow_np = np.ones((128, 128), BFNP)
    rep_np = np.zeros((8, 128), np.float32)
    for e in range(8):
        rep_np[e, 16 * e:16 * e + 16] = 2.0
    cvec_np = np.tile(np.array([[-CEXP, 0.0, 1.0, EPS]], np.float32), (128, 1))

    inv = (1.0 / ROPE_BASE ** (np.arange(0, 128, 2, dtype=np.float64) / 128.0))
    x = np.asarray(L0_attn_x, f32)

    in_maps = []
    for core in range(8):
        b, p = core // 2, core % 2
        ownrows = np.concatenate([np.arange(128 * (2 * j + p), 128 * (2 * j + p) + 128)
                                  for j in range(8)])
        xoc = np.ascontiguousarray(x[b][ownrows])
        pos = ownrows.astype(np.float64)
        ang = pos[None, :] * inv[:, None]
        cosv = np.cos(ang).astype(f32)
        sinv = np.sin(ang).astype(f32)
        crop_np = np.ascontiguousarray(np.concatenate([cosv, cosv], 0))
        srop_np = np.ascontiguousarray(np.concatenate([-sinv, sinv], 0))
        if p == 0:
            m0_np = tri.astype(BFNP)
            m1_np = np.zeros((128, 128), BFNP)
        else:
            m0_np = np.ones((128, 128), BFNP)
            m1_np = tri.astype(BFNP)
        in_maps.append({
            "xo": xoc, "wq": wq_r, "wk": wk_r, "wv": wv_r, "wo": wo_r,
            "bwt": bwt_r, "la": la_r, "lb": lb_r, "rwt": rwt_r,
            "crop": crop_np, "srop": srop_np, "m0": m0_np, "m1": m1_np,
            "identb": identb_np, "identf": identf_np, "ones128": ones128_np,
            "onesrow": onesrow_np, "rep": rep_np, "cvec": cvec_np,
        })
    return in_maps


def kernel(L0_attn_x, L0_attn_Wq, L0_attn_Wk, L0_attn_Wv, L0_attn_Wo,
           L0_ffn_base_W, L0_ffn_router_W, L0_ffn_lora_A, L0_ffn_lora_B):
    in_maps = make_inputs(L0_attn_x, L0_attn_Wq, L0_attn_Wk, L0_attn_Wv, L0_attn_Wo,
                          L0_ffn_base_W, L0_ffn_router_W, L0_ffn_lora_A, L0_ffn_lora_B)
    nc = _get_nc("full")
    res = run_bass_kernel_spmd(nc, in_maps, core_ids=list(range(8)))
    full = np.empty((B, T, D), np.float32)
    for core in range(8):
        b, p = core // 2, core % 2
        o = res.results[core]["out"]
        for j in range(8):
            gb = 2 * j + p
            full[b, 128 * gb:128 * gb + 128, :] = o[128 * j:128 * j + 128, :]
    return full


# version 07
# speedup vs baseline: 1.1374x; 1.0746x over previous
import numpy as np
import ml_dtypes
from contextlib import ExitStack

import concourse.bass as bass
import concourse.mybir as mybir
from concourse.bass_utils import run_bass_kernel_spmd
from concourse.alu_op_type import AluOpType as Alu

F32 = mybir.dt.float32
BF16 = mybir.dt.bfloat16
AF = mybir.ActivationFunctionType
AX = mybir.AxisListType.X
BFNP = ml_dtypes.bfloat16

B, T, D = 4, 2048, 4096
CEXP = 32.0
SCALE = 1.0 / float(np.sqrt(128.0))
EPS = 1e-5
ROPE_BASE = 500000.0

JMIN = [g // 2 for g in range(16)]
CHUNKS = []
for g in range(16):
    off = 128 * JMIN[g]
    if off < 512:
        CHUNKS.append((g, off, 512 - off, True))
        CHUNKS.append((g, 512, 512, False))
    else:
        CHUNKS.append((g, off, 1024 - off, True))
NCH = len(CHUNKS)
CH_OF_G = {}
for k, (g, off, n, first) in enumerate(CHUNKS):
    CH_OF_G.setdefault(g, []).append(k)
LAST_K = {0: CH_OF_G[7][0], 1: NCH - 1}


def sview(tsr, off, tail, nparts=128):
    ap = tsr[:, :]
    return bass.AP(ap.tensor, ap.offset + off, [[ap.ap[0][0], nparts]] + tail)


def build_nc(upto="full"):
    nc = bass.Bass("TRN2", target_bir_lowering=False, detect_race_conditions=False)

    def di(name, shape, dt=F32):
        return nc.dram_tensor(name, shape, dt, kind="ExternalInput").ap()

    xo = di("xo", [1024, D])
    wq = di("wq", [32, 128, 32, 128])
    wk = di("wk", [8, 128, 32, 128])
    wv = di("wv", [8, 128, 32, 128])
    wo = di("wo", [32, 128, 32, 128])
    bwt = di("bwt", [32, 128, 32, 128])
    la = di("la", [128, 32, 128])
    lb = di("lb", [128, 32, 128])
    rwt = di("rwt", [128, 32, 8])
    crop = di("crop", [128, 1024])
    srop = di("srop", [128, 1024])
    m0 = di("m0", [128, 128], BF16)
    m1 = di("m1", [128, 128], BF16)
    identb = di("identb", [128, 128], BF16)
    identf = di("identf", [128, 128], F32)
    ones128 = di("ones128", [128, 128], BF16)
    onesrow = di("onesrow", [128, 128], BF16)
    rep = di("rep", [8, 128], F32)
    cvec = di("cvec", [128, 4], F32)

    dbgA = (upto == "A")
    dbgB = (upto == "B")
    dbgC = (upto == "C")
    kc_d = nc.dram_tensor("kc_d", [8, 128, 1024], BF16, kind="Internal").ap()
    vc_d = nc.dram_tensor("vc_d", [8, 128, 1024], BF16, kind="Internal").ap()
    kg_d = nc.dram_tensor("kg_d", [2, 8, 128, 1024], BF16, kind="Internal").ap()
    vg_d = nc.dram_tensor("vg_d", [2, 8, 128, 1024], BF16, kind="Internal").ap()
    kvg_o = (nc.dram_tensor("kvg_o", [2, 16, 128, 1024], BF16, kind="ExternalOutput").ap()
             if dbgA else None)
    ctx_d = sg_d = out = wbig_d = mid_d = None
    if upto != "A":
        ctx_d = nc.dram_tensor("ctx_d", [32, 128, 1024], BF16,
                               kind="ExternalOutput" if dbgB else "Internal").ap()
    if upto in ("full", "C"):
        sg_d = nc.dram_tensor("sg_d", [32, 128, 1024], BF16,
                              kind="ExternalOutput" if dbgC else "Internal").ap()
        out = nc.dram_tensor("out", [1024, D], F32, kind="ExternalOutput").ap()
        if dbgC:
            wbig_d = nc.dram_tensor("wbig_d", [128, 1024], F32, kind="ExternalOutput").ap()
            mid_d = nc.dram_tensor("mid_d", [128, 1024], BF16, kind="ExternalOutput").ap()

    est = ExitStack()
    with est:
        sems = {}
        for name in ["scst", "scstb", "sx0", "sx1", "sveps", "sstd", "sln", "stpg", "stpe",
                     "skvmm", "skev", "svev", "skd0", "skd1", "swkv0", "swkv1", "svtg",
                     "svte", "svd0", "svd1", "sbar", "scck", "sccv",
                     "swq0", "swq1", "sq", "sqr", "ssmm", "sexp", "smsk",
                     "sav", "scs", "srec", "sbc", "sctx", "sctd0", "sctd1", "sblk", "sblv", "sbld",
                     "sd0", "sd0dma", "swo0", "swo1", "sbw0", "sbw1", "swof", "ssC", "sfC",
                     "ssev", "ssil", "ssg", "ssgd0", "ssgd1", "sc0",
                     "ssgi0", "ssgi1", "st2", "soadd", "sout0", "sout1"]:
            sems[name] = est.enter_context(nc.semaphore(name))
        S = sems

        def gsb(name, shape, dt):
            return est.enter_context(nc.sbuf_tensor(name, shape, dt))

        m0_s = gsb("m0_s", [128, 128], BF16)
        m1_s = gsb("m1_s", [128, 128], BF16)
        identb_s = gsb("identb_s", [128, 128], BF16)
        identf_s = gsb("identf_s", [128, 128], F32)
        ones_s = gsb("ones_s", [128, 128], BF16)
        onesr_s = gsb("onesr_s", [128, 128], BF16)
        rep_s = gsb("rep_s", [8, 128], F32)
        cvec_s = gsb("cvec_s", [128, 4], F32)
        CB_EXP = cvec_s[:, 0:1]
        CB_ZERO = cvec_s[:, 1:2]
        CB_EPS = cvec_s[:, 3:4]
        nc.const_aps.register(F32, 0.0, cvec_s[:, 1:2])
        nc.const_aps.register(F32, 1.0, cvec_s[:, 2:3])
        xnfm = gsb("xnfm", [128, 32 * 1024], BF16)
        lb_s = gsb("lb_s", [128, 4096], BF16)
        mid_s = gsb("mid_s", [128, 1024], BF16)
        stR = ExitStack()
        crop_s = stR.enter_context(nc.sbuf_tensor("crop_s", [128, 1024], F32))
        srop_s = stR.enter_context(nc.sbuf_tensor("srop_s", [128, 1024], F32))

        pb = [est.enter_context(nc.psum_tensor("pb%d" % i, [128, 512], F32))
              for i in range(8)]

        CONST_DMAS = 8

        stD = ExitStack()
        wbig_s = stD.enter_context(nc.sbuf_tensor("wbig_s", [128, 1024], F32))
        la_s = stD.enter_context(nc.sbuf_tensor("la_s", [128, 4096], BF16))
        rwt_s = stD.enter_context(nc.sbuf_tensor("rwt_s", [128, 256], BF16))
        lg_sb = stD.enter_context(nc.sbuf_tensor("lg_sb", [8, 1024], F32))
        lgt_sb = stD.enter_context(nc.sbuf_tensor("lgt_sb", [128, 64], F32))
        wdt_sb = stD.enter_context(nc.sbuf_tensor("wdt_sb", [128, 64], F32))
        wdf_sb = stD.enter_context(nc.sbuf_tensor("wdf_sb", [8, 1024], F32))
        srt8 = stD.enter_context(nc.sbuf_tensor("srt8", [128, 64], F32))
        m1bb = stD.enter_context(nc.sbuf_tensor("m1bb", [128, 64], F32))
        m2bb = stD.enter_context(nc.sbuf_tensor("m2bb", [128, 64], F32))
        d1b = stD.enter_context(nc.sbuf_tensor("d1b", [128, 64], F32))
        d2b = stD.enter_context(nc.sbuf_tensor("d2b", [128, 64], F32))
        m2nn = stD.enter_context(nc.sbuf_tensor("m2nn", [128, 64], F32))
        recs = stD.enter_context(nc.sbuf_tensor("recs", [128, 8], F32))
        eeW = stD.enter_context(nc.sbuf_tensor("eeW", [128, 8], F32))
        denW = stD.enter_context(nc.sbuf_tensor("denW", [128, 8], F32))
        dtsW = stD.enter_context(nc.sbuf_tensor("dtsW", [128, 8], F32))
        w1W = stD.enter_context(nc.sbuf_tensor("w1W", [128, 8], F32))
        w2W = stD.enter_context(nc.sbuf_tensor("w2W", [128, 8], F32))
        e1s = stD.enter_context(nc.sbuf_tensor("e1s", [128, 64], F32))
        e2s = stD.enter_context(nc.sbuf_tensor("e2s", [128, 64], F32))
        tts = stD.enter_context(nc.sbuf_tensor("tts", [128, 64], F32))

        stA = ExitStack()
        with stA:
            def asb(name, shape, dt):
                return stA.enter_context(nc.sbuf_tensor(name, shape, dt))
            x_in = [asb("x_in%d" % i, [128, D], F32) for i in range(2)]
            xn_tm = [asb("xn_tm%d" % i, [128, D], BF16) for i in range(2)]
            bno = asb("bno", [128, 48], F32)
            mvv = asb("mvv", [128, 2], F32)
            stdv = asb("stdv", [128, 1], F32)
            rstd = asb("rstd", [128, 1], F32)
            wkv_s = [asb("wkv_s%d" % i, [128, 4096], BF16) for i in range(2)]
            kst = [asb("kst%d" % i, [128, 1024], BF16) for i in range(2)]
            vfm = asb("vfm", [128, 8 * 1024], BF16)
            vst = [asb("vst%d" % i, [128, 1024], BF16) for i in range(2)]
            ra = asb("ra", [128, 512], F32)
            rb = asb("rb", [128, 512], F32)
            rs = asb("rs", [128, 512], F32)

            with nc.Block() as blk:
                @blk.sync
                def _(s):
                    for (dst, src) in [(cvec_s, cvec), (identb_s, identb),
                                       (m0_s, m0), (m1_s, m1), (identf_s, identf),
                                       (ones_s, ones128), (onesr_s, onesrow),
                                       (rep_s, rep)]:
                        s.dma_start(dst[:, :], src).then_inc(S["scst"], 16)
                    for j in range(2):
                        s.dma_start(x_in[j % 2][:, :], xo[128 * j:128 * j + 128, :]
                                    ).then_inc(S["sx%d" % (j % 2)], 16)
                    for (dst, src) in [(crop_s, crop), (srop_s, srop)]:
                        s.dma_start(dst[:, :], src).then_inc(S["scstb"], 16)
                    for j in range(2, 8):
                        s.wait_ge(S["sln"], j - 1)
                        s.dma_start(x_in[j % 2][:, :], xo[128 * j:128 * j + 128, :]
                                    ).then_inc(S["sx%d" % (j % 2)], 16)
                    for g in range(8):
                        s.wait_ge(S["skev"], g + 1)
                        s.dma_start(kc_d[g], kst[g % 2][:, :]
                                    ).then_inc(S["skd%d" % (g % 2)], 16)
                    for j in range(8):
                        s.wait_ge(S["svte"], j + 1)
                        s.dma_start(vc_d[j], vst[j % 2][:, :]
                                    ).then_inc(S["svd%d" % (j % 2)], 16)

                @blk.gpsimd
                def _(g_):
                    for goc in range(16):
                        if goc >= 2:
                            g_.wait_ge(S["skvmm"], 2 * (goc - 1))
                        src4 = wk[goc] if goc < 8 else wv[goc - 8]
                        src = bass.AP(src4.tensor, src4.offset,
                                      [[src4.ap[0][0], 128], [1, 4096]])
                        g_.dma_start(wkv_s[goc % 2][:, :], src,
                                     max_dma_last_dim=8192).then_inc(S["swkv%d" % (goc % 2)], 16)
                    g_.dma_start(rwt_s[:, :],
                                 bass.AP(rwt.tensor, rwt.offset, [[rwt.ap[0][0], 128], [1, 256]]),
                                 max_dma_last_dim=8192).then_inc(S["sd0dma"], 16)
                    g_.dma_start(la_s[:, :],
                                 bass.AP(la.tensor, la.offset, [[la.ap[0][0], 128], [1, 4096]]),
                                 max_dma_last_dim=8192).then_inc(S["sd0dma"], 16)
                    g_.dma_start(lb_s[:, :],
                                 bass.AP(lb.tensor, lb.offset, [[lb.ap[0][0], 128], [1, 4096]]),
                                 max_dma_last_dim=8192).then_inc(S["sd0dma"], 16)
                    g_.wait_ge(S["skd0"], 64)
                    g_.wait_ge(S["skd1"], 64)
                    g_.collective_compute(
                        "AllGather", mybir.AluOpType.bypass,
                        replica_groups=[[0, 1], [2, 3], [4, 5], [6, 7]],
                        ins=[kc_d.opt()], outs=[kg_d.opt()],
                    ).then_inc(S["scck"], 1)
                    g_.wait_ge(S["svd0"], 64)
                    g_.wait_ge(S["svd1"], 64)
                    g_.collective_compute(
                        "AllGather", mybir.AluOpType.bypass,
                        replica_groups=[[0, 1], [2, 3], [4, 5], [6, 7]],
                        ins=[vc_d.opt()], outs=[vg_d.opt()],
                    ).then_inc(S["sccv"], 1)

                @blk.vector
                def _(v):
                    v.wait_ge(S["scst"], 16 * CONST_DMAS)

                    def ln(j):
                        v.wait_ge(S["sx%d" % (j % 2)], 16 * (j // 2 + 1))
                        xs = x_in[j % 2]
                        for u in range(8):
                            v.bn_stats(bno[:, 6 * u:6 * u + 6], xs[:, 512 * u:512 * u + 512])
                        v.drain()
                        v.bn_aggr(mvv[:, :], bno[:, :])
                        v.drain().then_inc(S["sveps"], 1)
                        v.wait_ge(S["sstd"], j + 1)
                        v.reciprocal(rstd[:, :], stdv[:, :])
                        v.drain()
                        if j >= 2:
                            v.wait_ge(S["stpg"], 8 * (j - 1))
                        v.tensor_scalar(xn_tm[j % 2][:, :], xs[:, :], mvv[:, 0:1],
                                        rstd[:, 0:1], Alu.subtract, Alu.mult
                                        ).then_inc(S["sln"], 1)

                    def ropeK(g):
                        if g == 0:
                            v.wait_ge(S["scstb"], 32)
                        pair = (pb[2], pb[3]) if g % 2 == 0 else (pb[4], pb[5])
                        if g >= 2:
                            v.wait_ge(S["skd%d" % (g % 2)], 16 * (g // 2))
                        for th in range(2):
                            v.wait_ge(S["skvmm"], 2 * g + th + 1)
                            ps = pair[th]
                            v.tensor_tensor(ra[:, :], ps[:, :],
                                            crop_s[:, 512 * th:512 * th + 512], Alu.mult)
                            v.tensor_copy(rs[0:64, :], ps[64:128, :])
                            v.tensor_copy(rs[64:128, :], ps[0:64, :])
                            v.tensor_tensor(rb[:, :], rs[:, :],
                                            srop_s[:, 512 * th:512 * th + 512], Alu.mult)
                            v.tensor_tensor(kst[g % 2][:, 512 * th:512 * th + 512],
                                            ra[:, :], rb[:, :], Alu.add)
                        v.drain().then_inc(S["skev"], 1)

                    def vstev(j):
                        v.wait_ge(S["svtg"], j + 1)
                        if j >= 2:
                            v.wait_ge(S["svd%d" % (j % 2)], 16 * (j // 2))
                        v.tensor_copy(vst[j % 2][:, 0:512], pb[6][:, :])
                        v.tensor_copy(vst[j % 2][:, 512:1024], pb[7][:, :])
                        v.drain().then_inc(S["svte"], 1)

                    for j in range(8):
                        ln(j)
                    for g in range(8):
                        ropeK(g)
                    for j in range(8):
                        vstev(j)

                @blk.scalar
                def _(a):
                    a.wait_ge(S["scst"], 16 * CONST_DMAS)

                    def sqrt_(j):
                        a.wait_ge(S["sveps"], j + 1)
                        a.activation(stdv[:, :], mvv[:, 1:2], AF.Sqrt,
                                     bias=CB_EPS, scale=1.0)
                        a.drain().then_inc(S["sstd"], 1)

                    def tevac_sub(j):
                        for grp in range(8):
                            gc = 8 * j + grp
                            a.wait_ge(S["stpg"], gc + 1)
                            dst = sview(xnfm, (4 * grp) * 1024 + 128 * j,
                                        [[1024, 4], [1, 128]])
                            a.copy(dst, pb[gc % 2][:, :]).then_inc(S["stpe"], 1)

                    sqrt_(0)
                    sqrt_(1)
                    for j in range(8):
                        tevac_sub(j)
                        if j + 2 <= 7:
                            sqrt_(j + 2)
                    for g in range(8):
                        pair = (pb[2], pb[3]) if g % 2 == 0 else (pb[4], pb[5])
                        for th in range(2):
                            a.wait_ge(S["skvmm"], 2 * (8 + g) + th + 1)
                            inst = a.copy(vfm[:, g * 1024 + 512 * th:g * 1024 + 512 * th + 512],
                                          pair[th][:, :])
                            if th == 1:
                                inst.then_inc(S["svev"], 1)

                @blk.tensor
                def _(t):
                    t.wait_ge(S["scst"], 16 * CONST_DMAS)
                    for j in range(8):
                        t.wait_ge(S["sln"], j + 1)
                        for grp in range(8):
                            gc = 8 * j + grp
                            if gc >= 2:
                                t.wait_ge(S["stpe"], gc - 1)
                            unit = pb[gc % 2]
                            inst = None
                            for u in range(4):
                                kf = 4 * grp + u
                                inst = t.matmul(unit[:, 128 * u:128 * u + 128],
                                                xn_tm[j % 2][:, 128 * kf:128 * kf + 128],
                                                identb_s[:, :], start=True, stop=True)
                            inst.then_inc(S["stpg"], 1)
                    t.wait_ge(S["stpe"], 64)
                    for goc in range(16):
                        t.wait_ge(S["swkv%d" % (goc % 2)], 16 * (goc // 2 + 1))
                        pgoc = goc - 2
                        if pgoc >= 0:
                            if pgoc < 8:
                                t.wait_ge(S["skev"], pgoc + 1)
                            else:
                                t.wait_ge(S["svev"], pgoc - 8 + 1)
                        pair = (pb[2], pb[3]) if goc % 2 == 0 else (pb[4], pb[5])
                        for th in range(2):
                            inst = None
                            for kf in range(32):
                                inst = t.matmul(
                                    pair[th][:, :],
                                    wkv_s[goc % 2][:, 128 * kf:128 * kf + 128],
                                    xnfm[:, 1024 * kf + 512 * th:1024 * kf + 512 * th + 512],
                                    start=(kf == 0), stop=(kf == 31))
                            inst.then_inc(S["skvmm"], 1)
                    t.wait_ge(S["svev"], 8)
                    for j in range(8):
                        if j >= 1:
                            t.wait_ge(S["svte"], j)
                        inst = None
                        for g in range(8):
                            unit = pb[6] if g < 4 else pb[7]
                            inst = t.matmul(unit[:, 128 * (g % 4):128 * (g % 4) + 128],
                                            vfm[:, g * 1024 + 128 * j:g * 1024 + 128 * j + 128],
                                            identb_s[:, :], start=True, stop=True)
                        inst.then_inc(S["svtg"], 1)

                @blk.sync
                def _(s):
                    s.nop().then_inc(S["sbar"], 1)

                @blk.gpsimd
                def _(g_):
                    g_.nop().then_inc(S["sbar"], 1)

                @blk.vector
                def _(v):
                    v.nop().then_inc(S["sbar"], 1)

                @blk.scalar
                def _(a):
                    a.nop().then_inc(S["sbar"], 1)

                @blk.tensor
                def _(t):
                    t.nop().then_inc(S["sbar"], 1)

        if upto == "A":
            with nc.Block() as blk:
                @blk.sync
                def _(s):
                    s.wait_ge(S["sbar"], 5)
                    s.wait_ge(S["scck"], 1)
                    s.wait_ge(S["sccv"], 1)
                    s.dma_start(kvg_o[:, 0:8], kg_d).then_inc(S["sbld"], 16)
                    s.dma_start(kvg_o[:, 8:16], vg_d).then_inc(S["sbld"], 16)
                    s.wait_ge(S["sbld"], 32)
            stD.close()
            return nc

        d0 = [0]

        def d0w(eng):
            eng.wait_ge(S["sd0"], d0[0])

        def d0i(eng):
            d0[0] += 1
            eng.drain().then_inc(S["sd0"], 1)

        with nc.Block() as blk:
            @blk.tensor
            def _(t):
                t.wait_ge(S["sbar"], 5)
                t.wait_ge(S["sd0dma"], 48)
                t.drain()
                for th in range(2):
                    inst = None
                    for kf in range(32):
                        inst = t.matmul(pb[th][0:8, :], rwt_s[:, 8 * kf:8 * kf + 8],
                                        xnfm[:, 1024 * kf + 512 * th:1024 * kf + 512 * th + 512],
                                        start=(kf == 0), stop=(kf == 31))
                    d0i(t)

            @blk.vector
            def _(v):
                v.wait_ge(S["sbar"], 5)
                d0w(v)
                v.tensor_copy(lg_sb[0:8, 0:512], pb[0][0:8, :])
                v.tensor_copy(lg_sb[0:8, 512:1024], pb[1][0:8, :])
                d0i(v)

            @blk.tensor
            def _(t):
                d0w(t)
                for i in range(8):
                    t.matmul(pb[2][:, 8 * i:8 * i + 8],
                             lg_sb[0:8, 128 * i:128 * i + 128],
                             identf_s[0:8, 0:8], start=True, stop=True)
                d0i(t)

            @blk.vector
            def _(v):
                d0w(v)
                v.tensor_copy(lgt_sb[:, :], pb[2][:, 0:64])
                d0i(v)

            @blk.vector
            def _(v):
                d0w(v)
                for tb in range(8):
                    v.max(out=srt8[:, 8 * tb:8 * tb + 8], in_=lgt_sb[:, 8 * tb:8 * tb + 8])
                v.drain()
                for tb in range(8):
                    blkl = lgt_sb[:, 8 * tb:8 * tb + 8]
                    v.tensor_scalar(m1bb[:, 8 * tb:8 * tb + 8], blkl, 0.0,
                                    srt8[:, 8 * tb:8 * tb + 1], Alu.mult, Alu.add)
                    v.tensor_scalar(m2bb[:, 8 * tb:8 * tb + 8], blkl, 0.0,
                                    srt8[:, 8 * tb + 1:8 * tb + 2], Alu.mult, Alu.add)
                    v.tensor_tensor(dtsW[:, tb:tb + 1], srt8[:, 8 * tb + 1:8 * tb + 2],
                                    srt8[:, 8 * tb:8 * tb + 1], Alu.subtract)
                v.drain()
                v.tensor_tensor(d1b[:, :], lgt_sb[:, :], m1bb[:, :], Alu.subtract)
                v.tensor_tensor(d2b[:, :], lgt_sb[:, :], m2bb[:, :], Alu.subtract)
                v.tensor_tensor(m2nn[:, :], m2bb[:, :], lgt_sb[:, :], Alu.subtract)
                v.drain()
                v.tensor_scalar(d1b[:, :], d1b[:, :], 1e6, -100.0, Alu.mult, Alu.max)
                v.tensor_tensor(d2b[:, :], d2b[:, :], m2nn[:, :], Alu.min)
                v.drain()
                v.tensor_scalar(d2b[:, :], d2b[:, :], 1e6, -100.0, Alu.mult, Alu.max)
                d0i(v)

            @blk.scalar
            def _(a):
                d0w(a)
                a.activation(e1s[:, :], d1b[:, :], AF.Exp, bias=CB_ZERO, scale=1.0)
                a.activation(e2s[:, :], d2b[:, :], AF.Exp, bias=CB_ZERO, scale=1.0)
                a.activation(eeW[:, :], dtsW[:, :], AF.Exp, bias=CB_ZERO, scale=1.0)
                d0i(a)

            @blk.vector
            def _(v):
                d0w(v)
                v.tensor_scalar(denW[:, :], eeW[:, :], 1.0, None, Alu.add)
                v.drain()
                v.reciprocal(recs[:, :], denW[:, :])
                v.drain()
                v.tensor_tensor(w2W[:, :], eeW[:, :], recs[:, :], Alu.mult)
                v.drain()
                v.tensor_scalar(w1W[:, :], w2W[:, :], -1.0, 1.0, Alu.mult, Alu.add)
                v.drain()
                for tb in range(8):
                    v.tensor_scalar(tts[:, 8 * tb:8 * tb + 8], e1s[:, 8 * tb:8 * tb + 8],
                                    w1W[:, tb:tb + 1], None, Alu.mult)
                v.drain()
                for tb in range(8):
                    v.scalar_tensor_tensor(wdt_sb[:, 8 * tb:8 * tb + 8],
                                           e2s[:, 8 * tb:8 * tb + 8],
                                           w2W[:, tb:tb + 1], tts[:, 8 * tb:8 * tb + 8],
                                           Alu.mult, Alu.add)
                d0i(v)

            @blk.tensor
            def _(t):
                d0w(t)
                for i in range(8):
                    unit = pb[3] if i < 4 else pb[6]
                    t.matmul(unit[0:8, 128 * (i % 4):128 * (i % 4) + 128],
                             wdt_sb[:, 8 * i:8 * i + 8], identf_s[:, :],
                             start=True, stop=True)
                d0i(t)

            @blk.vector
            def _(v):
                d0w(v)
                for i in range(4):
                    v.tensor_copy(wdf_sb[0:8, 128 * i:128 * i + 128],
                                  pb[3][0:8, 128 * i:128 * i + 128])
                for i in range(4):
                    v.tensor_copy(wdf_sb[0:8, 512 + 128 * i:512 + 128 * i + 128],
                                  pb[6][0:8, 128 * i:128 * i + 128])
                d0i(v)
            D0_WDF = d0[0]

            @blk.tensor
            def _(t):
                d0w(t)
                for th in range(2):
                    t.matmul(pb[4 + th][:, :], rep_s[:, :],
                             wdf_sb[0:8, 512 * th:512 * th + 512],
                             start=True, stop=True)
                d0i(t)

            @blk.vector
            def _(v):
                d0w(v)
                v.tensor_copy(wbig_s[:, 0:512], pb[4][:, :])
                v.tensor_copy(wbig_s[:, 512:1024], pb[5][:, :])
                d0i(v)

            @blk.tensor
            def _(t):
                d0w(t)
                for th in range(2):
                    inst = None
                    for kf in range(32):
                        inst = t.matmul(pb[2 + th][:, :], la_s[:, 128 * kf:128 * kf + 128],
                                        xnfm[:, 1024 * kf + 512 * th:1024 * kf + 512 * th + 512],
                                        start=(kf == 0), stop=(kf == 31))
                    d0i(t)

            @blk.vector
            def _(v):
                d0w(v)
                v.tensor_tensor(mid_s[:, 0:512], pb[2][:, :], wbig_s[:, 0:512], Alu.mult)
                v.tensor_tensor(mid_s[:, 512:1024], pb[3][:, :],
                                wbig_s[:, 512:1024], Alu.mult)
                d0i(v)

        D0_FINAL = d0[0]
        stD.close()

        stB = ExitStack()
        with stB:
            def bsb(name, shape, dt):
                return stB.enter_context(nc.sbuf_tensor(name, shape, dt))
            k_sb = bsb("k_sb", [128, 8 * 2048], BF16)
            v_sb = bsb("v_sb", [128, 16 * 1024], BF16)
            wq_s = [bsb("wq_s%d" % i, [128, 4096], BF16) for i in range(2)]
            q_sb = [bsb("q_sb%d" % i, [128, 1024], BF16) for i in range(2)]
            p_sb = [bsb("p_sb%d" % i, [128, 1024], BF16) for i in range(2)]
            ra2 = bsb("ra2", [128, 512], F32)
            rb2 = bsb("rb2", [128, 512], F32)
            rs2 = bsb("rs2", [128, 512], F32)
            rz = bsb("rz", [128, 1024], BF16)
            cte = bsb("cte", [128, 1024], F32)
            cxs = [bsb("cxs%d" % i, [128, 1024], BF16) for i in range(2)]

            with nc.Block() as blk:
                @blk.gpsimd
                def _(g_):
                    for h in range(32):
                        if h >= 2:
                            g_.wait_ge(S["sq"], 2 * (h - 1))
                        src = bass.AP(wq[h].tensor, wq[h].offset,
                                      [[wq[h].ap[0][0], 128], [1, 4096]])
                        g_.dma_start(wq_s[h % 2][:, :], src,
                                     max_dma_last_dim=8192).then_inc(S["swq%d" % (h % 2)], 16)

                @blk.sync
                def _(s):
                    s.wait_ge(S["sbar"], 5)
                    s.wait_ge(S["scck"], 1)

                    def kload(g_):
                        for r in range(2):
                            src = bass.AP(kg_d.tensor,
                                          kg_d.offset + (8 * r + g_) * (128 * 1024),
                                          [[1024, 128], [128, 8], [1, 128]])
                            dst = sview(k_sb, 2048 * g_ + 128 * r, [[256, 8], [1, 128]])
                            s.dma_start(dst, src).then_inc(S["sblk"], 16)

                    def vload(G):
                        src = bass.AP(vg_d.tensor,
                                      vg_d.offset + (8 * (G % 2) + G // 2) * (128 * 1024),
                                      [[1024, 128], [1, 1024]])
                        dst = sview(v_sb, 1024 * G, [[1, 1024]])
                        s.dma_start(dst, src).then_inc(S["sblv"], 16)

                    kload(0)
                    s.wait_ge(S["sccv"], 1)
                    for g_ in range(1, 4):
                        for G in range(4 * g_ - 4, 4 * g_):
                            vload(G)
                        kload(g_)
                    for G in range(12, 16):
                        vload(G)
                    for g_ in range(4, 8):
                        kload(g_)
                    for h in range(32):
                        s.wait_ge(S["sctx"], h + 1)
                        s.dma_start(ctx_d[h], cxs[h % 2][:, :]).then_inc(S["sctd%d" % (h % 2)], 16)

                SUNIT = [0, 1, 6]

                @blk.tensor
                def _(t):
                    t.wait_ge(S["sd0"], D0_WDF)

                    def qproj(h):
                        t.wait_ge(S["swq%d" % (h % 2)], 16 * (h // 2 + 1))
                        for th in range(2):
                            inst = None
                            for kf in range(32):
                                inst = t.matmul(
                                    pb[6 + th][:, :],
                                    wq_s[h % 2][:, 128 * kf:128 * kf + 128],
                                    xnfm[:, 1024 * kf + 512 * th:1024 * kf + 512 * th + 512],
                                    start=(kf == 0), stop=(kf == 31))
                            inst.then_inc(S["sq"], 1)

                    qproj(0)
                    for h in range(32):
                        gq = h // 4
                        t.wait_ge(S["sblk"], 32 * (gq + 1))
                        if h == 2:
                            t.wait_ge(S["sblv"], 16 * 16)
                        t.wait_ge(S["sqr"], h + 1)
                        if h >= 1:
                            t.wait_ge(S["sctx"], h)
                        started_r = set()
                        started_c = set()

                        def s_mm(k):
                            g_, off, n, first = CHUNKS[k]
                            ci = NCH * h + k
                            if ci >= 3:
                                t.wait_ge(S["sexp"], ci - 2)
                            t.matmul(pb[SUNIT[ci % 3]][:, 0:n],
                                     k_sb[:, 2048 * gq + 128 * g_:2048 * gq + 128 * g_ + 128],
                                     q_sb[h % 2][:, off:off + n], start=True, stop=True
                                     ).then_inc(S["ssmm"], 1)

                        def cs_av(k):
                            g_, off, n, first = CHUNKS[k]
                            ci = NCH * h + k
                            if ci == 0:
                                t.wait_ge(S["sd0"], D0_FINAL)
                            if first:
                                t.wait_ge(S["smsk"], 16 * h + g_ + 1)
                            else:
                                t.wait_ge(S["sexp"], ci + 1)
                            u = off // 512
                            o2 = off % 512
                            inst = t.matmul(pb[4 + u][:, o2:o2 + n], ones_s[:, :],
                                            p_sb[g_ % 2][:, off:off + n],
                                            start=(u not in started_r),
                                            stop=(k == LAST_K[u]))
                            started_r.add(u)
                            if k == LAST_K[u]:
                                inst.then_inc(S["scs"], 1)
                            if h <= 1:
                                t.wait_ge(S["sblv"], 16 * (g_ + 1))
                            inst = t.matmul(
                                pb[2 + u][:, o2:o2 + n],
                                v_sb[:, 1024 * g_ + 128 * gq:1024 * g_ + 128 * gq + 128],
                                p_sb[g_ % 2][:, off:off + n],
                                start=(u not in started_c),
                                stop=(k == LAST_K[u]))
                            started_c.add(u)
                            inst.then_inc(S["sav"], 1)

                        s_mm(0)
                        s_mm(1)
                        for k in range(2, NCH):
                            s_mm(k)
                            cs_av(k - 2)
                        cs_av(NCH - 2)
                        cs_av(NCH - 1)
                        t.wait_ge(S["srec"], h + 1)
                        t.wait_ge(S["sexp"], NCH * (h + 1))
                        inst = None
                        for u in range(2):
                            inst = t.matmul(pb[u][:, :], onesr_s[:, :],
                                            rz[:, 512 * u:512 * u + 512],
                                            start=True, stop=True)
                        inst.then_inc(S["sbc"], 1)
                        if h + 1 < 32:
                            qproj(h + 1)

                @blk.scalar
                def _(a):
                    for h in range(32):
                        for k in range(NCH):
                            g_, off, n, first = CHUNKS[k]
                            ci = NCH * h + k
                            a.wait_ge(S["ssmm"], ci + 1)
                            if g_ >= 2:
                                a.wait_ge(S["sav"], NCH * h + CH_OF_G[g_ - 2][-1] + 1)
                            elif h >= 1:
                                a.wait_ge(S["sav"], NCH * (h - 1) + CH_OF_G[g_ + 14][-1] + 1)
                            a.activation(p_sb[g_ % 2][:, off:off + n],
                                         pb[[0, 1, 6][ci % 3]][:, 0:n],
                                         AF.Exp, bias=CB_EXP, scale=SCALE
                                         ).then_inc(S["sexp"], 1)

                @blk.vector
                def _(v):
                    v.wait_ge(S["scstb"], 32)
                    v.memset(rz[:, :], 0.0)
                    for h in range(32):
                        for th in range(2):
                            v.wait_ge(S["sq"], 2 * h + th + 1)
                            ps = pb[6 + th]
                            v.tensor_tensor(ra2[:, :], ps[:, :],
                                            crop_s[:, 512 * th:512 * th + 512], Alu.mult)
                            v.tensor_copy(rs2[0:64, :], ps[64:128, :])
                            v.tensor_copy(rs2[64:128, :], ps[0:64, :])
                            v.tensor_tensor(rb2[:, :], rs2[:, :],
                                            srop_s[:, 512 * th:512 * th + 512], Alu.mult)
                            v.tensor_tensor(q_sb[h % 2][:, 512 * th:512 * th + 512],
                                            ra2[:, :], rb2[:, :], Alu.add)
                        v.drain().then_inc(S["sqr"], 1)
                        for g_ in range(16):
                            kf_ = CH_OF_G[g_][0]
                            off = CHUNKS[kf_][1]
                            v.wait_ge(S["sexp"], NCH * h + kf_ + 1)
                            mt = m0_s if g_ % 2 == 0 else m1_s
                            v.tensor_tensor(p_sb[g_ % 2][:, off:off + 128],
                                            p_sb[g_ % 2][:, off:off + 128],
                                            mt[:, :], Alu.mult)
                            v.drain().then_inc(S["smsk"], 1)
                        if h >= 1:
                            v.wait_ge(S["sbc"], h)
                        v.wait_ge(S["scs"], 2 * h + 1)
                        with nc.allow_low_precision(reason="1/r bcast in bf16: per-column scale, rel 2^-9"):
                            v.reciprocal(rz[0:1, 0:512], pb[4][0:1, :])
                        v.wait_ge(S["scs"], 2 * h + 2)
                        with nc.allow_low_precision(reason="1/r bcast in bf16: per-column scale, rel 2^-9"):
                            v.reciprocal(rz[0:1, 512:1024], pb[5][0:1, :])
                        v.drain().then_inc(S["srec"], 1)
                        v.wait_ge(S["sav"], NCH * (h + 1))
                        v.wait_ge(S["sbc"], h + 1)
                        if h >= 2:
                            v.wait_ge(S["sctd%d" % (h % 2)], 16 * (h // 2))
                        v.tensor_copy(cte[:, 0:512], pb[2][:, :])
                        v.tensor_copy(cte[:, 512:1024], pb[3][:, :])
                        v.tensor_tensor(cxs[h % 2][:, 0:512], cte[:, 0:512],
                                        pb[0][:, :], Alu.mult)
                        v.tensor_tensor(cxs[h % 2][:, 512:1024], cte[:, 512:1024],
                                        pb[1][:, :], Alu.mult)
                        v.drain().then_inc(S["sctx"], 1)

                @blk.sync
                def _(s):
                    s.nop().then_inc(S["sbar"], 1)

                @blk.gpsimd
                def _(g_):
                    g_.nop().then_inc(S["sbar"], 1)

                @blk.vector
                def _(v):
                    v.nop().then_inc(S["sbar"], 1)

                @blk.scalar
                def _(a):
                    a.nop().then_inc(S["sbar"], 1)

                @blk.tensor
                def _(t):
                    t.nop().then_inc(S["sbar"], 1)

        if upto == "B":
            with nc.Block() as blk:
                @blk.sync
                def _(s):
                    s.wait_ge(S["sbar"], 10)
                    s.wait_ge(S["sctd0"], 16 * 16)
                    s.wait_ge(S["sctd1"], 16 * 16)
            return nc

        stR.close()
        stC = ExitStack()
        with stC:
            def csb(name, shape, dt):
                return stC.enter_context(nc.sbuf_tensor(name, shape, dt))
            ctx_s = csb("ctx_s", [128, 32 * 1024], BF16)
            wo_s = [csb("wo_s%d" % i, [128, 4096], BF16) for i in range(2)]
            bwt_s = [csb("bwt_s%d" % i, [128, 4096], BF16) for i in range(2)]
            s_sb = csb("s_sb", [128, 1024], BF16)
            g_sb = [csb("g_sb%d" % i, [128, 1024], BF16) for i in range(2)]
            sg_sb = [csb("sg_sb%d" % i, [128, 1024], BF16) for i in range(2)]

            with nc.Block() as blk:
                @blk.gpsimd
                def _(g_):
                    for oc in range(32):
                        if oc >= 2:
                            g_.wait_ge(S["swof"], oc - 1)
                        g_.dma_start(wo_s[oc % 2][:, :],
                                     bass.AP(wo[oc].tensor, wo[oc].offset,
                                             [[wo[oc].ap[0][0], 128], [1, 4096]]),
                                     max_dma_last_dim=8192).then_inc(S["swo%d" % (oc % 2)], 16)
                        g_.dma_start(bwt_s[oc % 2][:, :],
                                     bass.AP(bwt[oc].tensor, bwt[oc].offset,
                                             [[bwt[oc].ap[0][0], 128], [1, 4096]]),
                                     max_dma_last_dim=8192).then_inc(S["sbw%d" % (oc % 2)], 16)

                @blk.sync
                def _(s):
                    s.wait_ge(S["sbar"], 10)
                    s.wait_ge(S["sctd0"], 16 * 16)
                    s.wait_ge(S["sctd1"], 16 * 16)
                    s.dma_start(ctx_s[:, :].rearrange("p (a c) -> p a c", a=32),
                                ctx_d.rearrange("a p c -> p a c")).then_inc(S["sc0"], 16)
                    if dbgC:
                        s.dma_start(mid_d, mid_s[:, :]).then_inc(S["sc0"], 16)
                    for oc in range(32):
                        s.wait_ge(S["ssg"], oc + 1)
                        s.dma_start(sg_d[oc], sg_sb[oc % 2][:, :]).then_inc(S["ssgd%d" % (oc % 2)], 16)

                @blk.tensor
                def _(t):
                    t.wait_ge(S["sbar"], 10)
                    t.wait_ge(S["sc0"], 16)
                    for oc in range(32):
                        t.wait_ge(S["swo%d" % (oc % 2)], 16 * (oc // 2 + 1))
                        t.wait_ge(S["sbw%d" % (oc % 2)], 16 * (oc // 2 + 1))
                        if oc >= 2:
                            t.wait_ge(S["ssev"], oc - 1)
                            t.wait_ge(S["ssil"], 2 * (oc - 1))
                        spair = (pb[0], pb[1]) if oc % 2 == 0 else (pb[4], pb[5])
                        fpair = (pb[2], pb[3]) if oc % 2 == 0 else (pb[6], pb[7])
                        for th in range(2):
                            inst = None
                            for kf in range(32):
                                inst = t.matmul(
                                    spair[th][:, :], wo_s[oc % 2][:, 128 * kf:128 * kf + 128],
                                    ctx_s[:, 1024 * kf + 512 * th:1024 * kf + 512 * th + 512],
                                    start=(kf == 0), stop=(kf == 31))
                            inst.then_inc(S["ssC"], 1)
                        for th in range(2):
                            for kf in range(32):
                                t.matmul(
                                    fpair[th][:, :], bwt_s[oc % 2][:, 128 * kf:128 * kf + 128],
                                    xnfm[:, 1024 * kf + 512 * th:1024 * kf + 512 * th + 512],
                                    start=(kf == 0), stop=False)
                            t.matmul(fpair[th][:, :], lb_s[:, 128 * oc:128 * oc + 128],
                                     mid_s[:, 512 * th:512 * th + 512],
                                     start=False, stop=True).then_inc(S["sfC"], 1)
                        t.nop().then_inc(S["swof"], 1)

                @blk.scalar
                def _(a):
                    for oc in range(32):
                        fpair = (pb[2], pb[3]) if oc % 2 == 0 else (pb[6], pb[7])
                        if oc >= 2:
                            a.wait_ge(S["ssg"], oc - 1)
                        for th in range(2):
                            a.wait_ge(S["sfC"], 2 * oc + th + 1)
                            a.activation(g_sb[oc % 2][:, 512 * th:512 * th + 512],
                                         fpair[th][:, :], AF.Silu, bias=CB_ZERO,
                                         scale=1.0).then_inc(S["ssil"], 1)

                @blk.vector
                def _(v):
                    for oc in range(32):
                        spair = (pb[0], pb[1]) if oc % 2 == 0 else (pb[4], pb[5])
                        v.wait_ge(S["ssC"], 2 * oc + 2)
                        v.tensor_copy(s_sb[:, 0:512], spair[0][:, :])
                        v.tensor_copy(s_sb[:, 512:1024], spair[1][:, :])
                        v.drain().then_inc(S["ssev"], 1)
                        v.wait_ge(S["ssil"], 2 * oc + 2)
                        if oc >= 2:
                            v.wait_ge(S["ssgd%d" % (oc % 2)], 16 * (oc // 2))
                        v.tensor_tensor(sg_sb[oc % 2][:, :], s_sb[:, :], g_sb[oc % 2][:, :],
                                        Alu.mult)
                        v.drain().then_inc(S["ssg"], 1)

                @blk.sync
                def _(s):
                    s.nop().then_inc(S["sbar"], 1)

                @blk.gpsimd
                def _(g_):
                    g_.nop().then_inc(S["sbar"], 1)

                @blk.vector
                def _(v):
                    v.nop().then_inc(S["sbar"], 1)

                @blk.scalar
                def _(a):
                    a.nop().then_inc(S["sbar"], 1)

                @blk.tensor
                def _(t):
                    t.nop().then_inc(S["sbar"], 1)

        stE = ExitStack()
        with stE:
            def esb(name, shape, dt):
                return stE.enter_context(nc.sbuf_tensor(name, shape, dt))
            sgi = [esb("sgi%d" % i, [128, 4096], BF16) for i in range(2)]
            xr = [esb("xr%d" % i, [128, 4096], F32) for i in range(2)]
            ob = [esb("ob%d" % i, [128, 4096], F32) for i in range(2)]

            with nc.Block() as blk:
                @blk.sync
                def _(s):
                    s.wait_ge(S["sbar"], 15)
                    s.wait_ge(S["ssgd0"], 16 * 16)
                    s.wait_ge(S["ssgd1"], 16 * 16)

                    def out_dma(tb):
                        s.wait_ge(S["soadd"], 8 * (tb + 1))
                        s.dma_start(out[128 * tb:128 * tb + 128, :], ob[tb % 2][:, :]
                                    ).then_inc(S["sout%d" % (tb % 2)], 16)

                    for tb in range(8):
                        if tb >= 2:
                            s.wait_ge(S["st2"], 8 * (tb - 1))
                        for half in range(2):
                            src = sg_d[16 * half:16 * half + 16, :, 128 * tb:128 * tb + 128]
                            s.dma_start(
                                sgi[tb % 2][:, 2048 * half:2048 * half + 2048].rearrange(
                                    "p (a c) -> p a c", a=16),
                                src.rearrange("a p c -> p a c")).then_inc(S["ssgi%d" % (tb % 2)], 16)
                        if tb >= 2:
                            s.wait_ge(S["soadd"], 8 * (tb - 1))
                        s.dma_start(xr[tb % 2][:, :], xo[128 * tb:128 * tb + 128, :]
                                    ).then_inc(S["ssgi%d" % (tb % 2)], 16)
                        if tb >= 2:
                            out_dma(tb - 2)
                    out_dma(6)
                    out_dma(7)
                    s.wait_ge(S["sout0"], 16 * 4)
                    s.wait_ge(S["sout1"], 16 * 4)

                @blk.tensor
                def _(t):
                    t.wait_ge(S["sbar"], 15)
                    for tb in range(8):
                        t.wait_ge(S["ssgi%d" % (tb % 2)], 48 * (tb // 2 + 1))
                        for grp in range(8):
                            if tb >= 1:
                                t.wait_ge(S["soadd"], 8 * (tb - 1) + grp + 1)
                            unit = pb[grp]
                            inst = None
                            for u in range(4):
                                oc = 4 * grp + u
                                inst = t.matmul(unit[:, 128 * u:128 * u + 128],
                                                sgi[tb % 2][:, 128 * oc:128 * oc + 128],
                                                identb_s[:, :], start=True, stop=True)
                            inst.then_inc(S["st2"], 1)

                @blk.vector
                def _(v):
                    for tb in range(8):
                        for grp in range(8):
                            cum = 8 * tb + grp
                            v.wait_ge(S["st2"], cum + 1)
                            if tb >= 2 and grp == 0:
                                v.wait_ge(S["sout%d" % (tb % 2)], 16 * (tb // 2))
                            v.tensor_tensor(ob[tb % 2][:, 512 * grp:512 * grp + 512],
                                            xr[tb % 2][:, 512 * grp:512 * grp + 512],
                                            pb[grp][:, :], Alu.add).then_inc(S["soadd"], 1)

    return nc


_NC_CACHE = {}


def _get_nc(upto="full"):
    if upto not in _NC_CACHE:
        _NC_CACHE[upto] = build_nc(upto)
    return _NC_CACHE[upto]


def make_inputs(L0_attn_x, L0_attn_Wq, L0_attn_Wk, L0_attn_Wv, L0_attn_Wo,
                L0_ffn_base_W, L0_ffn_router_W, L0_ffn_lora_A, L0_ffn_lora_B):
    f32 = np.float32
    wq_r = np.ascontiguousarray(np.asarray(L0_attn_Wq, f32).reshape(32, 128, 32, 128).transpose(2, 1, 0, 3))
    wk_r = np.ascontiguousarray(np.asarray(L0_attn_Wk, f32).reshape(32, 128, 8, 128).transpose(2, 1, 0, 3))
    wv_r = np.ascontiguousarray(np.asarray(L0_attn_Wv, f32).reshape(32, 128, 8, 128).transpose(2, 1, 0, 3))
    wo_r = np.ascontiguousarray(np.asarray(L0_attn_Wo, f32).reshape(32, 128, 32, 128).transpose(2, 1, 0, 3))
    bwt_r = np.ascontiguousarray(np.asarray(L0_ffn_base_W, f32).T.reshape(32, 128, 32, 128).transpose(2, 1, 0, 3))
    la_r = np.ascontiguousarray(np.asarray(L0_ffn_lora_A, f32).transpose(1, 0, 2).reshape(32, 128, 128).transpose(1, 0, 2))
    lb_r = np.ascontiguousarray(np.asarray(L0_ffn_lora_B, f32).reshape(128, 32, 128))
    rwt_r = np.ascontiguousarray(np.asarray(L0_ffn_router_W, f32).T.reshape(32, 128, 8).transpose(1, 0, 2))

    tri = np.tril(np.ones((128, 128), np.float32)).T
    identb_np = np.eye(128, dtype=BFNP)
    identf_np = np.eye(128, dtype=np.float32)
    ones128_np = np.ones((128, 128), BFNP)
    onesrow_np = np.ones((128, 128), BFNP)
    rep_np = np.zeros((8, 128), np.float32)
    for e in range(8):
        rep_np[e, 16 * e:16 * e + 16] = 2.0
    cvec_np = np.tile(np.array([[-CEXP, 0.0, 1.0, EPS]], np.float32), (128, 1))

    inv = (1.0 / ROPE_BASE ** (np.arange(0, 128, 2, dtype=np.float64) / 128.0))
    x = np.asarray(L0_attn_x, f32)

    in_maps = []
    for core in range(8):
        b, p = core // 2, core % 2
        ownrows = np.concatenate([np.arange(128 * (2 * j + p), 128 * (2 * j + p) + 128)
                                  for j in range(8)])
        xoc = np.ascontiguousarray(x[b][ownrows])
        pos = ownrows.astype(np.float64)
        ang = pos[None, :] * inv[:, None]
        cosv = np.cos(ang).astype(f32)
        sinv = np.sin(ang).astype(f32)
        crop_np = np.ascontiguousarray(np.concatenate([cosv, cosv], 0))
        srop_np = np.ascontiguousarray(np.concatenate([-sinv, sinv], 0))
        if p == 0:
            m0_np = tri.astype(BFNP)
            m1_np = np.zeros((128, 128), BFNP)
        else:
            m0_np = np.ones((128, 128), BFNP)
            m1_np = tri.astype(BFNP)
        in_maps.append({
            "xo": xoc, "wq": wq_r, "wk": wk_r, "wv": wv_r, "wo": wo_r,
            "bwt": bwt_r, "la": la_r, "lb": lb_r, "rwt": rwt_r,
            "crop": crop_np, "srop": srop_np, "m0": m0_np, "m1": m1_np,
            "identb": identb_np, "identf": identf_np, "ones128": ones128_np,
            "onesrow": onesrow_np, "rep": rep_np, "cvec": cvec_np,
        })
    return in_maps


def kernel(L0_attn_x, L0_attn_Wq, L0_attn_Wk, L0_attn_Wv, L0_attn_Wo,
           L0_ffn_base_W, L0_ffn_router_W, L0_ffn_lora_A, L0_ffn_lora_B):
    in_maps = make_inputs(L0_attn_x, L0_attn_Wq, L0_attn_Wk, L0_attn_Wv, L0_attn_Wo,
                          L0_ffn_base_W, L0_ffn_router_W, L0_ffn_lora_A, L0_ffn_lora_B)
    nc = _get_nc("full")
    res = run_bass_kernel_spmd(nc, in_maps, core_ids=list(range(8)))
    full = np.empty((B, T, D), np.float32)
    for core in range(8):
        b, p = core // 2, core % 2
        o = res.results[core]["out"]
        for j in range(8):
            gb = 2 * j + p
            full[b, 128 * gb:128 * gb + 128, :] = o[128 * j:128 * j + 128, :]
    return full
